# version 03
# baseline (speedup 1.0000x reference)
import numpy as np
import ml_dtypes
import concourse.bass as bass
import concourse.mybir as mybir
from concourse.bass_utils import run_bass_kernel_spmd

F32 = mybir.dt.float32
BF16 = mybir.dt.bfloat16
I32 = mybir.dt.int32
AF = mybir.ActivationFunctionType
ALU = mybir.AluOpType
AX = mybir.AxisListType

B, T, C, INNER = 4, 4096, 1024, 4096
OWN = 2048
NW = float(INNER * C)
EPS = 1e-5
NEG = -1e30
THETA = 10000.0

J0 = [0, 3, 4, 7, 8, 11, 12, 15]
J1 = [1, 2, 5, 6, 9, 10, 13, 14]

ENGS = ("sync", "vector", "scalar", "tensor", "gpsimd")


class Pool:
    def __init__(self, ap):
        self.ap = ap
        self.cols = ap.shape[1]
        self.off = 0
        self.stack = []

    def alloc(self, n):
        assert self.off + n <= self.cols, f"pool overflow {self.off}+{n}>{self.cols}"
        v = self.ap[:, self.off:self.off + n]
        self.off += n
        return v

    def push(self):
        self.stack.append(self.off)

    def pop(self):
        self.off = self.stack.pop()


class Prog:

    def __init__(self, nc, block):
        self.nc, self.block = nc, block
        self.ops = {e: [] for e in ENGS}
        self.sem = {}
        self.loadsems = []
        self.storesems = []
        for name in ("sV", "sA", "sT", "bS", "bG"):
            h = nc.semaphore(name)
            self.sem[name] = [h.__enter__(), 0]

    def dma(self, eng, fn, stream):
        name = "q:" + stream
        if name not in self.sem:
            h = self.nc.semaphore(name)
            self.sem[name] = [h.__enter__(), 0]
            (self.loadsems if eng == "sync" else self.storesems).append(name)
        return self.ins(eng, fn, inc=name)

    def raw(self, eng, fn):
        self.ops[eng].append(fn)

    def ins(self, eng, fn, inc=None):
        if inc is None:
            self.ops[eng].append(fn)
            return None
        s = self.sem[inc]
        step = 16 if inc.startswith("q:") else 1
        s[1] += step
        h = s[0]
        self.ops[eng].append(lambda e, fn=fn, h=h, step=step: fn(e).then_inc(h, step))
        return s[1]

    def wait(self, eng, semname, val=None):
        if semname not in self.sem:
            return
        s = self.sem[semname]
        v = s[1] if val is None else val
        if v > 0:
            self.ops[eng].append(lambda e, h=s[0], v=v: e.wait_ge(h, v))

    def cnt(self, semname):
        return self.sem[semname][1]

    def barrier(self):
        for nm in self.loadsems:
            self.wait("sync", nm)
        for nm in self.storesems:
            self.wait("gpsimd", nm)
        marks = {}
        for eng, s in (("sync", "bS"), ("vector", "sV"), ("scalar", "sA"),
                       ("tensor", "sT"), ("gpsimd", "bG")):
            marks[eng] = self.ins(eng, lambda e: e.drain(), inc=s)
        for eng in ENGS:
            for oeng, s in (("sync", "bS"), ("vector", "sV"), ("scalar", "sA"),
                            ("tensor", "sT"), ("gpsimd", "bG")):
                if oeng != eng:
                    self.wait(eng, s, marks[oeng])

    def flush(self):
        for eng in ENGS:
            fns = self.ops[eng]
            self.ops[eng] = []
            if not fns:
                continue
            dec = getattr(self.block, eng)

            def section(e, fns=fns):
                for f in fns:
                    f(e)
            dec(section)


def _rearr_pc(ap, p=128):
    return ap.rearrange("(n p) m -> p n m", p=p)


def build_nc(phases=5, p5stage=3):
    nc = bass.Bass("TRN2", target_bir_lowering=False)

    xb = nc.dram_tensor("xb", [T, C], F32, kind="ExternalInput").ap()
    xown = nc.dram_tensor("xown", [OWN, C], F32, kind="ExternalInput").ap()
    tkch = nc.dram_tensor("tkch", [T, C], F32, kind="ExternalInput").ap()
    tkcl = nc.dram_tensor("tkcl", [T, C], BF16, kind="ExternalInput").ap()
    tksh = nc.dram_tensor("tksh", [T, C], F32, kind="ExternalInput").ap()
    tksl = nc.dram_tensor("tksl", [T, C], BF16, kind="ExternalInput").ap()
    tqch = nc.dram_tensor("tqch", [OWN, C], F32, kind="ExternalInput").ap()
    tqcl = nc.dram_tensor("tqcl", [OWN, C], BF16, kind="ExternalInput").ap()
    tqsh = nc.dram_tensor("tqsh", [OWN, C], F32, kind="ExternalInput").ap()
    tqsl = nc.dram_tensor("tqsl", [OWN, C], BF16, kind="ExternalInput").ap()
    wq = nc.dram_tensor("wq", [C, C], F32, kind="ExternalInput").ap()
    wk = nc.dram_tensor("wk", [C, C], F32, kind="ExternalInput").ap()
    wv = nc.dram_tensor("wv", [C, C], F32, kind="ExternalInput").ap()
    wo = nc.dram_tensor("wo", [C, C], F32, kind="ExternalInput").ap()
    bqv = nc.dram_tensor("bqv", [1, C], F32, kind="ExternalInput").ap()
    bkv = nc.dram_tensor("bkv", [1, C], F32, kind="ExternalInput").ap()
    bvv = nc.dram_tensor("bvv", [1, C], F32, kind="ExternalInput").ap()
    bov = nc.dram_tensor("bov", [1, C], F32, kind="ExternalInput").ap()
    bopv = nc.dram_tensor("bopv", [128, 8], F32, kind="ExternalInput").ap()
    wgT = nc.dram_tensor("wgT", [C, INNER], F32, kind="ExternalInput").ap()
    wuT = nc.dram_tensor("wuT", [C, INNER], F32, kind="ExternalInput").ap()
    wdT = nc.dram_tensor("wdT", [INNER, C], F32, kind="ExternalInput").ap()
    maskseq = nc.dram_tensor("maskseq", [16, 128, 512], F32, kind="ExternalInput").ap()
    identv = nc.dram_tensor("identv", [128, 128], F32, kind="ExternalInput").ap()
    ones2d = nc.dram_tensor("ones2d", [128, 128], F32, kind="ExternalInput").ap()
    ones1r = nc.dram_tensor("ones1r", [1, 128], F32, kind="ExternalInput").ap()
    out = nc.dram_tensor("out", [OWN, C], F32, kind="ExternalOutput").ap()


    Ks = nc.dram_tensor("Ks", [C, T], F32).ap()
    Vs = nc.dram_tensor("Vs", [T, C], F32).ap()
    Qs = nc.dram_tensor("Qs", [C, OWN], F32).ap()
    XQs = nc.dram_tensor("XQs", [C, OWN], BF16).ap()
    Gq = nc.dram_tensor("Gq", [C, INNER], BF16).ap()
    Uq = nc.dram_tensor("Uq", [C, INNER], BF16).ap()
    Dq = nc.dram_tensor("Dq", [INNER, C], BF16).ap()
    Pt = nc.dram_tensor("Pt", [OWN, C], F32).ap()

    from contextlib import ExitStack
    es = ExitStack()
    with es:
        pf_t = es.enter_context(nc.sbuf_tensor("pf", [128, 28672], F32))
        ph_t = es.enter_context(nc.sbuf_tensor("ph", [128, 32768], BF16))
        pi_t = es.enter_context(nc.sbuf_tensor("pi", [128, 2048], I32))
        sm_t = es.enter_context(nc.sbuf_tensor("sm", [128, 96], F32))
        idf = es.enter_context(nc.sbuf_tensor("idf", [128, 128], F32))
        idh = es.enter_context(nc.sbuf_tensor("idh", [128, 128], BF16))
        onx = es.enter_context(nc.sbuf_tensor("onx", [128, 128], F32))
        onr = es.enter_context(nc.sbuf_tensor("onr", [1, 128], F32))
        bop = es.enter_context(nc.sbuf_tensor("bop", [128, 8], F32))
        pp0 = es.enter_context(nc.psum_tensor("pp0", [128, 2048], F32))
        pp1 = es.enter_context(nc.psum_tensor("pp1", [128, 1024], F32))
        pph = es.enter_context(nc.psum_tensor("pph", [128, 2048], BF16))
        block = es.enter_context(nc.Block())

        pf = Pool(pf_t.ap())
        ph = Pool(ph_t.ap())
        pi = Pool(pi_t.ap())
        sm = Pool(sm_t.ap())
        P = Prog(nc, block)

        idf_a, idh_a, onx_a, onr_a, bop_a = idf.ap(), idh.ap(), onx.ap(), onr.ap(), bop.ap()

        mwg = sm.alloc(1); mwu = sm.alloc(1); mwd = sm.alloc(1)
        ra_inv = sm.alloc(16)
        sgc = sm.alloc(16); suc = sm.alloc(16)

        P.dma("sync", lambda e: e.dma_start(out=idf_a[:], in_=identv), "lc")
        P.dma("sync", lambda e: e.dma_start(out=onx_a[:], in_=ones2d), "lc")
        P.dma("sync", lambda e: e.dma_start(out=onr_a[:], in_=ones1r), "lc")
        P.dma("sync", lambda e: e.dma_start(out=bop_a[:], in_=bopv), "lc")
        P.wait("vector", "q:lc")
        P.ins("vector", lambda e: e.tensor_copy(idh_a[:], idf_a[:]), inc="sV")

        pf.push(); ph.push()
        wr = [pf.alloc(4096), pf.alloc(4096)]
        wvq = pf.alloc(4096)
        wnn = pf.alloc(4096)
        acc = sm.alloc(1)
        tabs = sm.alloc(1)
        swb = sm.alloc(1)
        wqo = [ph.alloc(4096), ph.alloc(4096)]

        qw_lastuse = [None, None]
        qw_lastq = [None, None]

        def quant_weight(src, dst, n_pc, cols, mw_tile):
            src_r = _rearr_pc(src)
            dst_r = _rearr_pc(dst)
            P.wait("sync", "sV")
            P.wait("sync", "sA")
            P.wait("scalar", "sV")
            lastuse = qw_lastuse
            prevacc = None
            for pc in range(n_pc):
                sl = pc % 2
                if lastuse[sl] is not None:
                    P.wait("sync", "sA", lastuse[sl])
                ld = P.dma("sync", lambda e, pc=pc, sl=sl: e.dma_start(
                    out=wr[sl][:, :cols], in_=src_r[:, pc, :]), "wr%d" % sl)
                P.wait("scalar", "q:wr%d" % sl, ld)
                if prevacc is not None:
                    P.wait("scalar", "sV", prevacc)
                a = P.ins("scalar", lambda e, sl=sl: e.activation(
                    wvq[:, :cols], wr[sl][:, :cols], AF.Abs, accum_out=tabs), inc="sA")
                lastuse[sl] = a
                P.wait("vector", "sA", a)
                P.ins("vector", lambda e: e.drain())
                if pc == 0:
                    prevacc = P.ins("vector", lambda e: e.tensor_copy(acc, tabs), inc="sV")
                else:
                    prevacc = P.ins("vector", lambda e: e.tensor_tensor(
                        acc, acc, tabs, op=ALU.add), inc="sV")
            P.wait("tensor", "q:lc")
            P.wait("tensor", "sV", prevacc)
            tmk = P.ins("tensor", lambda e: e.matmul(
                pp0.ap()[:, 0:1], onx_a[:], acc, start=True, stop=True), inc="sT")
            P.wait("vector", "sT", tmk)
            P.ins("vector", lambda e: e.tensor_scalar(
                mw_tile, pp0.ap()[:, 0:1], 1.0 / NW, 1e-5, ALU.mult, ALU.max))
            P.ins("vector", lambda e: e.drain())
            P.ins("vector", lambda e: e.reciprocal(swb, mw_tile))
            swv = P.ins("vector", lambda e: e.drain(), inc="sV")
            lastq = qw_lastq
            for pc in range(n_pc):
                sl = pc % 2
                if lastuse[sl] is not None:
                    P.wait("sync", "sA", lastuse[sl])
                ld = P.dma("sync", lambda e, pc=pc, sl=sl: e.dma_start(
                    out=wr[sl][:, :cols], in_=src_r[:, pc, :]), "wr%d" % sl)
                P.wait("scalar", "q:wr%d" % sl, ld)
                P.wait("scalar", "sV", swv)
                P.wait("scalar", "sV")
                a = P.ins("scalar", lambda e, sl=sl: e.activation(
                    wvq[:, :cols], wr[sl][:, :cols], AF.Identity, scale=swb), inc="sA")
                lastuse[sl] = a
                P.wait("vector", "sA", a)
                P.ins("vector", lambda e: e.tensor_single_scalar(
                    wnn[:, :cols], wvq[:, :cols], -0.5, ALU.is_le))
                P.ins("vector", lambda e: e.drain())
                if lastq[sl] is not None:
                    P.wait("vector", "q:sw%d" % sl, lastq[sl])
                v = P.ins("vector", lambda e, sl=sl: e.scalar_tensor_tensor(
                    wqo[sl][:, :cols], wvq[:, :cols], 0.5, wnn[:, :cols],
                    ALU.is_ge, ALU.subtract), inc="sV")
                P.wait("gpsimd", "sV", v)
                lastq[sl] = P.dma("gpsimd", lambda e, pc=pc, sl=sl: e.dma_start(
                    out=dst_r[:, pc, :], in_=wqo[sl][:, :cols]), "sw%d" % sl)

        if phases >= 1:
            quant_weight(wgT, Gq, 8, 4096, mwg)
            quant_weight(wuT, Uq, 8, 4096, mwu)
            quant_weight(wdT, Dq, 32, 1024, mwd)
        pf.pop(); ph.pop()
        P.barrier()

        def emit_ln(xt, xnt, pjscr, vecs):
            rs = P.ins("vector", lambda e: e.reduce_sum(vecs["a"], xt, axis=AX.X), inc="sV")
            P.ins("vector", lambda e: e.drain())
            nm = P.ins("vector", lambda e: e.tensor_scalar_mul(
                vecs["b"], vecs["a"], -1.0 / C), inc="sV")
            P.wait("scalar", "sV", nm)
            sq = P.ins("scalar", lambda e: e.activation(
                pjscr, xt, AF.Square, bias=vecs["b"], scale=1.0,
                accum_out=vecs["c"]), inc="sA")
            P.wait("vector", "sA", sq)
            vv = P.ins("vector", lambda e: e.tensor_scalar(
                vecs["d"], vecs["c"], 1.0 / C, EPS, ALU.mult, ALU.add), inc="sV")
            P.wait("scalar", "sV", vv)
            sr = P.ins("scalar", lambda e: e.activation(vecs["e"], vecs["d"], AF.Sqrt), inc="sA")
            P.wait("vector", "sA", sr)
            P.ins("vector", lambda e: e.drain())
            P.ins("vector", lambda e: e.reciprocal(vecs["f"], vecs["e"]))
            P.ins("vector", lambda e: e.drain())
            bb = P.ins("vector", lambda e: e.tensor_tensor(
                vecs["g"], vecs["b"], vecs["f"], op=ALU.mult), inc="sV")
            P.wait("scalar", "sV", bb)
            return P.ins("scalar", lambda e: e.activation(
                xnt, xt, AF.Identity, scale=vecs["f"], bias=vecs["g"]), inc="sA")

        def emit_transpose(src, dst, ident, psview, wait_sem, wait_val, bf16=False):
            P.wait("tensor", wait_sem, wait_val)
            P.wait("tensor", "sA")
            tv = None
            for ccb in range(8):
                tv = P.ins("tensor", lambda e, ccb=ccb: e.transpose(
                    psview[:, ccb * 128:(ccb + 1) * 128],
                    src[:, ccb * 128:(ccb + 1) * 128], ident),
                    inc="sT" if ccb == 7 else None)
            P.wait("scalar", "sT", tv)
            return P.ins("scalar", lambda e: e.activation(dst, psview, AF.Copy), inc="sA")

        def emit_proj(xnf, wsb, brow, psview, wait_val):
            P.wait("tensor", "sA", wait_val)
            mmv = None
            for cb in range(2):
                for cin in range(8):
                    P.ins("tensor", lambda e, cb=cb, cin=cin: e.matmul(
                        psview[:, cb * 512:(cb + 1) * 512],
                        xnf[:, cin * 128:(cin + 1) * 128],
                        wsb[:, cin * 1024 + cb * 512: cin * 1024 + cb * 512 + 512],
                        start=(cin == 0), stop=False))
                mmv = P.ins("tensor", lambda e, cb=cb: e.matmul(
                    psview[:, cb * 512:(cb + 1) * 512], onr_a[:],
                    brow[:, cb * 512:(cb + 1) * 512], start=False, stop=True),
                    inc="sT" if cb == 1 else None)
            return mmv

        def emit_rope(pj, rp, ct, st, tswp, ready_val):
            P.wait("vector", "sA", ready_val)
            P.ins("vector", lambda e: e.tensor_tensor(rp, pj, ct, op=ALU.mult))
            P.ins("vector", lambda e: e.tensor_tensor(
                tswp[:, 0:1024:2], pj[:, 1:1024:2], st[:, 0:1024:2], op=ALU.mult))
            P.ins("vector", lambda e: e.tensor_tensor(
                tswp[:, 1:1024:2], pj[:, 0:1024:2], st[:, 1:1024:2], op=ALU.mult))
            P.ins("vector", lambda e: e.drain())
            return P.ins("vector", lambda e: e.tensor_tensor(
                rp, rp, tswp, op=ALU.add), inc="sV")

        pf.push(); ph.push()
        wk_sb = pf.alloc(8192)
        wv_sb = pf.alloc(8192)
        bkr = pf.alloc(1024).ap()[0:1, :] if False else None
        pf.pop()
        pf.push()
        wk_sb = pf.alloc(8192)
        wv_sb = pf.alloc(8192)
        xt = pf.alloc(1024)
        xnt = pf.alloc(1024)
        xnf = pf.alloc(1024)
        pj = pf.alloc(1024)
        rp = pf.alloc(1024)
        ct = pf.alloc(1024)
        stt = pf.alloc(1024)
        tlo_h = ph.alloc(1024)
        tswp = pf.alloc(1024)
        kfm = pf.alloc(1024)
        vecs = {k: sm.alloc(1) for k in "abcdefg"}
        br_k = es.enter_context(nc.sbuf_tensor("br_k", [1, 1024], F32)).ap()
        br_v = es.enter_context(nc.sbuf_tensor("br_v", [1, 1024], F32)).ap()
        br_q = es.enter_context(nc.sbuf_tensor("br_q", [1, 1024], F32)).ap()

        P.dma("sync", lambda e: e.dma_start(
            out=wk_sb.rearrange("p (cc n) -> p cc n", cc=8),
            in_=wk.rearrange("(cc p) n -> p cc n", p=128)), "lw")
        P.dma("sync", lambda e: e.dma_start(
            out=wv_sb.rearrange("p (cc n) -> p cc n", cc=8),
            in_=wv.rearrange("(cc p) n -> p cc n", p=128)), "lw")
        P.dma("sync", lambda e: e.dma_start(out=br_k, in_=bkv), "lw")
        P.dma("sync", lambda e: e.dma_start(out=br_v, in_=bvv), "lw")
        wload = P.cnt("q:lw")

        Ks_r = _rearr_pc(Ks)
        Vs_r = Vs.rearrange("(n p) c -> p n c", p=128)

        def prep_tile(t0, is_q, qi):
            xsrc = xown if is_q else xb
            th_c, tl_c = (tqch, tqcl) if is_q else (tkch, tkcl)
            th_s, tl_s = (tqsh, tqsl) if is_q else (tksh, tksl)
            P.wait("sync", "sA")
            P.wait("sync", "sV")
            P.wait("sync", "sT")
            P.wait("scalar", "q:sk")
            P.wait("scalar", "q:sv")
            P.wait("scalar", "q:sq")
            P.wait("scalar", "q:sxq")
            lx = P.dma("sync", lambda e: e.dma_start(out=xt, in_=xsrc[t0:t0 + 128, :]), "xt")
            l1 = P.dma("sync", lambda e: e.dma_start(out=ct, in_=th_c[t0:t0 + 128, :]), "ct")
            l2 = P.dma("sync", lambda e: e.dma_start(out=tlo_h, in_=tl_c[t0:t0 + 128, :]), "tlo")
            P.wait("vector", "q:ct", l1)
            P.wait("vector", "q:tlo", l2)
            a1 = P.ins("vector", lambda e: e.tensor_tensor(ct, ct, tlo_h, op=ALU.add), inc="sV")
            P.wait("sync", "sV", a1)
            l3 = P.dma("sync", lambda e: e.dma_start(out=stt, in_=th_s[t0:t0 + 128, :]), "stt")
            l4 = P.dma("sync", lambda e: e.dma_start(out=tlo_h, in_=tl_s[t0:t0 + 128, :]), "tlo")
            P.wait("vector", "q:stt", l3)
            P.wait("vector", "q:tlo", l4)
            tabv = P.ins("vector", lambda e: e.tensor_tensor(stt, stt, tlo_h, op=ALU.add), inc="sV")
            P.wait("vector", "q:xt", lx)
            lnv = emit_ln(xt, xnt, pj, vecs)
            xnfv = emit_transpose(xnt, xnf, idf_a, pp0.ap()[:, 0:1024], "sA", lnv)
            if not is_q:
                kq = emit_proj(xnf, wk_sb, br_k, pp1.ap()[:, 0:1024], xnfv)
                P.wait("scalar", "sT", kq)
                kev = P.ins("scalar", lambda e: e.activation(pj, pp1.ap()[:, 0:1024], AF.Copy), inc="sA")
                P.wait("vector", "sV", tabv)
                rpv = emit_rope(pj, rp, ct, stt, tswp, kev)
                ktv = emit_transpose(rp, kfm, idf_a, pp0.ap()[:, 1024:2048], "sV", rpv)
                P.wait("gpsimd", "sA", ktv)
                P.dma("gpsimd", lambda e, t0=t0: e.dma_start(
                    out=Ks_r[:, :, t0:t0 + 128],
                    in_=kfm.rearrange("p (cc t) -> p cc t", cc=8)), "sk")
                vq = emit_proj(xnf, wv_sb, br_v, pp1.ap()[:, 0:1024], kev)
                P.wait("scalar", "sT", vq)
                P.wait("scalar", "sV", rpv)
                vev = P.ins("scalar", lambda e: e.activation(pj, pp1.ap()[:, 0:1024], AF.Copy), inc="sA")
                P.wait("gpsimd", "sA", vev)
                P.dma("gpsimd", lambda e, t0=t0: e.dma_start(
                    out=Vs_r[:, t0 // 128, :], in_=pj), "sv")
            else:
                qq = emit_proj(xnf, wk_sb, br_q, pp1.ap()[:, 0:1024], xnfv)
                P.wait("scalar", "sT", qq)
                qev = P.ins("scalar", lambda e: e.activation(pj, pp1.ap()[:, 0:1024], AF.Copy), inc="sA")
                P.wait("vector", "sV", tabv)
                rpv = emit_rope(pj, rp, ct, stt, tswp, qev)
                qtv = emit_transpose(rp, kfm, idf_a, pp0.ap()[:, 1024:2048], "sV", rpv)
                P.wait("gpsimd", "sA", qtv)
                P.dma("gpsimd", lambda e, t0=t0: e.dma_start(
                    out=Qs.rearrange("(cc p) t -> p cc t", p=128)[:, :, t0:t0 + 128],
                    in_=kfm.rearrange("p (cc t) -> p cc t", cc=8)), "sq")
                P.ins("vector", lambda e: e.reduce_max(
                    vecs["a"], xnt, axis=AX.X, apply_absolute_value=True))
                P.ins("vector", lambda e: e.drain())
                P.ins("vector", lambda e: e.tensor_single_scalar(vecs["b"], vecs["a"], 1e-5, ALU.max))
                P.ins("vector", lambda e: e.drain())
                P.ins("vector", lambda e: e.reciprocal(vecs["c"], vecs["b"]))
                P.ins("vector", lambda e: e.drain())
                P.ins("vector", lambda e: e.tensor_scalar_mul(vecs["d"], vecs["c"], 127.0))
                P.ins("vector", lambda e, qi=qi: e.tensor_scalar_mul(
                    ra_inv[:, qi:qi + 1], vecs["b"], 1.0 / 127.0))
                P.ins("vector", lambda e: e.drain())
                iv = P.ins("vector", lambda e: e.tensor_scalar(
                    pi_t.ap()[:, 0:1024], xnt, vecs["d"], None, ALU.mult), inc="sV")
                P.wait("scalar", "sV", iv)
                rv = P.ins("scalar", lambda e: e.activation(
                    ct, pi_t.ap()[:, 0:1024], AF.Copy, bias=-1.0, scale=1.0), inc="sA")
                P.ins("vector", lambda e: e.drain())
                P.ins("vector", lambda e: e.scalar_tensor_tensor(
                    tswp, xnt, vecs["d"], pi_t.ap()[:, 0:1024], ALU.mult, ALU.subtract))
                P.wait("vector", "sA", rv)
                P.ins("vector", lambda e: e.drain())
                P.ins("vector", lambda e: e.scalar_tensor_tensor(
                    stt, tswp, 0.5, ct, ALU.is_ge, ALU.add))
                P.ins("vector", lambda e: e.drain())
                nqv = P.ins("vector", lambda e: e.scalar_tensor_tensor(
                    tlo_h, tswp, -0.5, stt, ALU.is_ge, ALU.add), inc="sV")
                P.wait("tensor", "sV", nqv)
                tv = None
                for ccb in range(8):
                    tv = P.ins("tensor", lambda e, ccb=ccb: e.transpose(
                        pph.ap()[:, ccb * 128:ccb * 128 + 128],
                        tlo_h[:, ccb * 128:ccb * 128 + 128], idh_a),
                        inc="sT" if ccb == 7 else None)
                P.wait("scalar", "sT", tv)
                xqe = P.ins("scalar", lambda e: e.activation(
                    ph_qt, pph.ap()[:, 0:1024], AF.Copy), inc="sA")
                P.wait("gpsimd", "sA", xqe)
                P.dma("gpsimd", lambda e, t0=t0: e.dma_start(
                    out=XQs.rearrange("(cc p) t -> p cc t", p=128)[:, :, t0:t0 + 128],
                    in_=ph_qt.rearrange("p (cc t) -> p cc t", cc=8)), "sxq")

        ph_qt = ph.alloc(1024)
        P.wait("tensor", "q:lw", wload)
        if phases >= 2:
            for i in range(32):
                prep_tile(128 * i, False, 0)
        P.barrier()

        P.dma("sync", lambda e: e.dma_start(
            out=wk_sb.rearrange("p (cc n) -> p cc n", cc=8),
            in_=wq.rearrange("(cc p) n -> p cc n", p=128)), "lw")
        P.dma("sync", lambda e: e.dma_start(out=br_q, in_=bqv), "lw")
        P.wait("tensor", "q:lw")
        if phases >= 3:
            for i in range(16):
                prep_tile(128 * i, True, i)
        pf.pop(); ph.pop()
        P.barrier()

        pf.push(); ph.push()
        gu_full = [pf.alloc(4096) for _ in range(4)]
        rA = pf.alloc(2048); rB = pf.alloc(2048); rC = pf.alloc(2048)
        gatebuf = pf.alloc(1024)
        poutb = [pf.alloc(1024), pf.alloc(1024)]
        xo = [pf.alloc(1024), pf.alloc(1024)]
        xqg = ph.alloc(4096)
        wgq_q = ph.alloc(8192)
        wuq_q = ph.alloc(8192)
        nqh = ph.alloc(2048)
        gu_im = ph.alloc(8192)
        wdq_s = [ph.alloc(1024), ph.alloc(1024)]
        v4 = {k: sm.alloc(1) for k in "hijk"}
        sdc = sm.alloc(2)

        P.ins("vector", lambda e: e.tensor_scalar(sgc, ra_inv, mwg, None, ALU.mult))
        P.ins("vector", lambda e: e.tensor_scalar(suc, ra_inv, mwu, None, ALU.mult), inc="sV")

        Gq_r = _rearr_pc(Gq)
        Uq_r = _rearr_pc(Uq)
        Dq_r = _rearr_pc(Dq)
        Pt_r = Pt.rearrange("(n p) c -> p n c", p=128)
        XQs_r = XQs.rearrange("(cc p) t -> p cc t", p=128)

        for g in (range(4) if phases >= 4 else []):
            P.wait("sync", "sT")
            lq = P.dma("sync", lambda e, g=g: e.dma_start(
                out=xqg.rearrange("p (cc t) -> p cc t", cc=8),
                in_=XQs_r[:, :, g * 512:(g + 1) * 512]), "xqg")
            guv = [None] * 4
            for iq in range(4):
                P.wait("sync", "sT")
                lw1 = P.dma("sync", lambda e, iq=iq: e.dma_start(
                    out=wgq_q.rearrange("p (cc n) -> p cc n", cc=8),
                    in_=Gq_r[:, :, iq * 1024:(iq + 1) * 1024]), "wg")
                lw2 = P.dma("sync", lambda e, iq=iq: e.dma_start(
                    out=wuq_q.rearrange("p (cc n) -> p cc n", cc=8),
                    in_=Uq_r[:, :, iq * 1024:(iq + 1) * 1024]), "wu")
                for k in range(4):
                    tc = 4 * g + k
                    P.wait("tensor", "q:wg", lw1)
                    P.wait("tensor", "q:wu", lw2)
                    if iq == 0 and k == 0:
                        P.wait("tensor", "q:xqg", lq)
                    P.wait("tensor", "sV")
                    P.wait("tensor", "sA")
                    mmv = None
                    for cin in range(8):
                        for half in range(2):
                            P.ins("tensor", lambda e, cin=cin, half=half, k=k: e.matmul(
                                pp0.ap()[:, half * 512:half * 512 + 512],
                                xqg[:, cin * 512 + k * 128: cin * 512 + k * 128 + 128],
                                wgq_q[:, cin * 1024 + half * 512: cin * 1024 + half * 512 + 512],
                                start=(cin == 0), stop=(cin == 7)))
                        for half in range(2):
                            mmv = P.ins("tensor", lambda e, cin=cin, half=half, k=k: e.matmul(
                                pp0.ap()[:, 1024 + half * 512: 1024 + half * 512 + 512],
                                xqg[:, cin * 512 + k * 128: cin * 512 + k * 128 + 128],
                                wuq_q[:, cin * 1024 + half * 512: cin * 1024 + half * 512 + 512],
                                start=(cin == 0), stop=(cin == 7)),
                                inc="sT" if (cin == 7 and half == 1) else None)
                    P.wait("scalar", "sT", mmv)
                    P.wait("scalar", "sV")
                    sg = P.ins("scalar", lambda e, tc=tc: e.activation(
                        gatebuf, pp0.ap()[:, 0:1024], AF.Silu,
                        scale=sgc[:, tc:tc + 1]), inc="sA")
                    P.wait("vector", "sA", sg)
                    guv[k] = P.ins("vector", lambda e, tc=tc, k=k, iq=iq: e.scalar_tensor_tensor(
                        gu_full[k][:, iq * 1024:(iq + 1) * 1024],
                        pp1.ap()[:, 0:1024] if False else pp0.ap()[:, 1024:2048],
                        suc[:, tc:tc + 1], gatebuf, ALU.mult, ALU.mult), inc="sV")
            for dg in range(2):
                for kk in range(2):
                    k = dg * 2 + kk
                    tc = 4 * g + k
                    P.wait("vector", "sV", guv[k])
                    P.ins("vector", lambda e, k=k: e.reduce_max(
                        v4["h"], gu_full[k], axis=AX.X, apply_absolute_value=True))
                    P.ins("vector", lambda e: e.drain())
                    P.ins("vector", lambda e: e.tensor_single_scalar(v4["i"], v4["h"], 1e-5, ALU.max))
                    P.ins("vector", lambda e: e.drain())
                    P.ins("vector", lambda e: e.reciprocal(v4["j"], v4["i"]))
                    P.ins("vector", lambda e: e.drain())
                    P.ins("vector", lambda e: e.tensor_scalar_mul(v4["k"], v4["j"], 127.0))
                    P.ins("vector", lambda e, kk=kk: e.tensor_scalar_mul(
                        sdc[:, kk:kk + 1], v4["i"], 1.0 / 127.0))
                    P.ins("vector", lambda e: e.drain())
                    P.ins("vector", lambda e, kk=kk: e.tensor_scalar(
                        sdc[:, kk:kk + 1], sdc[:, kk:kk + 1], mwd, None, ALU.mult))
                    for hh in range(2):
                        sl = slice(hh * 2048, hh * 2048 + 2048)
                        P.wait("vector", "sT")
                        iv = P.ins("vector", lambda e, k=k, sl=sl: e.tensor_scalar(
                            pi_t.ap()[:, 0:2048], gu_full[k][:, sl], v4["k"], None,
                            ALU.mult), inc="sV")
                        P.wait("scalar", "sV", iv)
                        rv = P.ins("scalar", lambda e: e.activation(
                            rA, pi_t.ap()[:, 0:2048], AF.Copy, bias=-1.0, scale=1.0), inc="sA")
                        P.ins("vector", lambda e: e.drain())
                        P.ins("vector", lambda e, k=k, sl=sl: e.scalar_tensor_tensor(
                            rB, gu_full[k][:, sl], v4["k"], pi_t.ap()[:, 0:2048],
                            ALU.mult, ALU.subtract))
                        P.wait("vector", "sA", rv)
                        P.ins("vector", lambda e: e.drain())
                        P.ins("vector", lambda e: e.scalar_tensor_tensor(
                            rC, rB, 0.5, rA, ALU.is_ge, ALU.add))
                        P.ins("vector", lambda e: e.drain())
                        nv = P.ins("vector", lambda e: e.scalar_tensor_tensor(
                            nqh, rB, -0.5, rC, ALU.is_ge, ALU.add), inc="sV")
                        P.wait("tensor", "sV", nv)
                        P.wait("tensor", "sA")
                        tv = None
                        for iic in range(16):
                            tv = P.ins("tensor", lambda e, iic=iic: e.transpose(
                                pph.ap()[:, iic * 128:iic * 128 + 128],
                                nqh[:, iic * 128:iic * 128 + 128], idh_a),
                                inc="sT" if iic == 15 else None)
                        P.wait("scalar", "sT", tv)
                        P.ins("scalar", lambda e, hh=hh, kk=kk: e.activation(
                            gu_im.rearrange("p (ic kv t) -> p ic kv t", ic=32, kv=2)
                            [:, hh * 16:hh * 16 + 16, kk, :],
                            pph.ap().rearrange("p (ic t) -> p ic t", ic=16),
                            AF.Copy), inc="sA")
                P.wait("sync", "sA")
                P.wait("sync", "sT")
                lxo = [None, None]
                for kk in range(2):
                    tc = 4 * g + dg * 2 + kk
                    lxo[kk] = P.dma("sync", lambda e, tc=tc, kk=kk: e.dma_start(
                        out=xo[kk], in_=xown[tc * 128:tc * 128 + 128, :]), "xo%d" % kk)
                gusA = P.cnt("sA")
                P.wait("tensor", "sA", gusA)
                mmd = None
                icmark = {}
                for ic in range(32):
                    sl2 = ic % 2
                    if ic >= 2:
                        P.wait("sync", "sT", icmark[ic - 1])
                    lw = P.dma("sync", lambda e, ic=ic, sl2=sl2: e.dma_start(
                        out=wdq_s[sl2], in_=Dq_r[:, ic, :]), "wd%d" % sl2)
                    P.wait("tensor", "q:wd%d" % sl2, lw)
                    for kk in range(2):
                        for cb in range(2):
                            mmd = P.ins("tensor", lambda e, ic=ic, kk=kk, cb=cb, sl2=sl2: e.matmul(
                                pp0.ap()[:, kk * 1024 + cb * 512: kk * 1024 + cb * 512 + 512],
                                gu_im[:, ic * 256 + kk * 128: ic * 256 + kk * 128 + 128],
                                wdq_s[sl2][:, cb * 512:cb * 512 + 512],
                                start=(ic == 0), stop=(ic == 31)),
                                inc="sT" if (kk == 1 and cb == 1) else None)
                    icmark[ic] = mmd
                P.wait("vector", "sT", mmd)
                P.wait("vector", "q:sp0")
                P.wait("vector", "q:sp1")
                for kk in range(2):
                    tc = 4 * g + dg * 2 + kk
                    P.wait("vector", "q:xo%d" % kk, lxo[kk])
                    pv = P.ins("vector", lambda e, kk=kk: e.scalar_tensor_tensor(
                        poutb[kk], pp0.ap()[:, kk * 1024:kk * 1024 + 1024],
                        sdc[:, kk:kk + 1], xo[kk], ALU.mult, ALU.add), inc="sV")
                    P.wait("gpsimd", "sV", pv)
                    P.dma("gpsimd", lambda e, tc=tc, kk=kk: e.dma_start(
                        out=Pt_r[:, tc, :], in_=poutb[kk]), "sp%d" % kk)
        pf.pop(); ph.pop()
        P.barrier()

        pf.push()
        S = [pf.alloc(4096), pf.alloc(4096)]
        Kb = [pf.alloc(2048), pf.alloc(2048)]
        Vb = [pf.alloc(2048), pf.alloc(2048)]
        Qp = pf.alloc(2048)
        PTs = pf.alloc(256)
        Pblk = [pf.alloc(256), pf.alloc(256)]
        msk = [pf.alloc(512), pf.alloc(512)]
        wo_sb = pf.alloc(8192)
        v5 = {k: sm.alloc(1) for k in "mnop"}
        nmv = sm.alloc(2)
        bias_e = sm.alloc(2)
        lrun = sm.alloc(2)
        lsb = sm.alloc(2)
        linv = sm.alloc(2)

        P.dma("sync", lambda e: e.dma_start(
            out=wo_sb.rearrange("p (cc n) -> p cc n", cc=8),
            in_=wo.rearrange("(cc p) n -> p cc n", p=128)), "lw")
        Vs_rd = Vs.rearrange("(n p) c -> p n c", p=128)

        AOfm = S[0][:, 0:2048]
        Ofm = S[1][:, 0:2048]
        onb = S[1][:, 2048:3072]
        ptl = [S[0][:, 2048:3072], S[0][:, 3072:4096]]
        outb = [Kb[0][:, 0:1024], Kb[1][:, 0:1024]]

        for p5 in (range(8) if phases >= 5 else []):
            L = 512 * (p5 + 1)
            nb = L // 256
            o0 = 256 * p5
            P.wait("sync", "sT")
            P.wait("sync", "q:so0")
            P.wait("sync", "q:so1")
            P.wait("scalar", "sV")
            lq = P.dma("sync", lambda e, o0=o0: e.dma_start(
                out=Qp.rearrange("p (cc t) -> p cc t", cc=8),
                in_=Qs.rearrange("(cc p) t -> p cc t", p=128)[:, :, o0:o0 + 256]), "qp")
            P.wait("sync", "sV")
            lm0 = P.dma("sync", lambda e, p5=p5: e.dma_start(
                out=msk[0], in_=maskseq[2 * p5, :, :]), "mk0")
            lm1 = P.dma("sync", lambda e, p5=p5: e.dma_start(
                out=msk[1], in_=maskseq[2 * p5 + 1, :, :]), "mk1")
            evs = [None, None]
            for b in range(nb):
                sl = b % 2
                P.wait("sync", "sT")
                lk = P.dma("sync", lambda e, b=b, sl=sl: e.dma_start(
                    out=Kb[sl].rearrange("p (cc t) -> p cc t", cc=8),
                    in_=Ks_r[:, :, b * 256:b * 256 + 256]), "kb%d" % sl)
                P.wait("tensor", "q:kb%d" % sl, lk)
                if b == 0:
                    P.wait("tensor", "q:qp", lq)
                for qt in range(2):
                    psl = (2 * b + qt) % 4
                    psS = pp0.ap()[:, psl * 512:psl * 512 + 256]
                    P.wait("tensor", "sA")
                    mv = None
                    for cc in range(8):
                        mv = P.ins("tensor", lambda e, cc=cc, qt=qt, sl=sl, psS=psS: e.matmul(
                            psS, Qp[:, cc * 256 + qt * 128: cc * 256 + qt * 128 + 128],
                            Kb[sl][:, cc * 256:cc * 256 + 256],
                            start=(cc == 0), stop=(cc == 7)),
                            inc="sT" if cc == 7 else None)
                    P.wait("scalar", "sT", mv)
                    evs[qt] = P.ins("scalar", lambda e, qt=qt, b=b, psS=psS: e.activation(
                        S[qt][:, b * 256:b * 256 + 256], psS, AF.Copy), inc="sA")
            for qt in range(2):
                P.wait("vector", "sA", evs[qt])
                P.wait("vector", "q:mk0", lm0)
                P.wait("vector", "q:mk1", lm1)
                P.ins("vector", lambda e, qt=qt, L=L: e.tensor_tensor(
                    S[qt][:, L - 512:L], S[qt][:, L - 512:L], msk[qt], op=ALU.add))
                P.ins("vector", lambda e: e.drain())
                P.ins("vector", lambda e, qt=qt, L=L: e.reduce_max(
                    nmv[:, qt:qt + 1], S[qt][:, 0:L], axis=AX.X, negate=True))
                P.ins("vector", lambda e: e.drain())
                P.ins("vector", lambda e, qt=qt: e.tensor_scalar_mul(
                    bias_e[:, qt:qt + 1], nmv[:, qt:qt + 1], 1.0 / 32.0), inc="sV")
            mbv = P.cnt("sV")
            if p5stage < 2:
                continue
            pte_prev = None
            pvmm_prev = None
            lsb_used = [None, None]
            for b in range(nb):
                sl = b % 2
                if pvmm_prev is not None:
                    P.wait("sync", "sT", pvmm_prev)
                lv = P.dma("sync", lambda e, b=b, sl=sl: e.dma_start(
                    out=Vb[sl].rearrange("p (n c) -> p n c", n=2),
                    in_=Vs_rd[:, 2 * b:2 * b + 2, :]), "vb%d" % sl)
                for qt in range(2):
                    P.wait("scalar", "sV", mbv)
                    if lsb_used[qt] is not None:
                        P.wait("scalar", "sV", lsb_used[qt])
                    P.wait("scalar", "sT")
                    ex = P.ins("scalar", lambda e, qt=qt, b=b: e.activation(
                        Pblk[qt], S[qt][:, b * 256:b * 256 + 256], AF.Exp,
                        bias=bias_e[:, qt:qt + 1], scale=1.0 / 32.0,
                        accum_out=lsb[:, qt:qt + 1]), inc="sA")
                    P.wait("vector", "sA", ex)
                    P.ins("vector", lambda e: e.drain())
                    if b == 0:
                        lsb_used[qt] = P.ins("vector", lambda e, qt=qt: e.tensor_copy(
                            lrun[:, qt:qt + 1], lsb[:, qt:qt + 1]), inc="sV")
                    else:
                        lsb_used[qt] = P.ins("vector", lambda e, qt=qt: e.tensor_tensor(
                            lrun[:, qt:qt + 1], lrun[:, qt:qt + 1], lsb[:, qt:qt + 1],
                            op=ALU.add), inc="sV")
                    P.wait("tensor", "sA", ex)
                    if pte_prev is not None:
                        P.wait("tensor", "sA", pte_prev)
                    tv = None
                    for ks in range(2):
                        tv = P.ins("tensor", lambda e, ks=ks, qt=qt: e.transpose(
                            pp1.ap()[:, ks * 128:ks * 128 + 128],
                            Pblk[qt][:, ks * 128:ks * 128 + 128], idf_a),
                            inc="sT" if ks == 1 else None)
                    P.wait("scalar", "sT", tv)
                    if pvmm_prev is not None:
                        P.wait("scalar", "sT", pvmm_prev)
                    pte_prev = P.ins("scalar", lambda e: e.activation(
                        PTs, pp1.ap()[:, 0:256], AF.Copy), inc="sA")
                    P.wait("tensor", "sA", pte_prev)
                    P.wait("tensor", "q:vb%d" % sl, lv)
                    mv = None
                    for ks in range(2):
                        for cb in range(2):
                            mv = P.ins("tensor", lambda e, ks=ks, cb=cb, qt=qt, sl=sl, b=b, nb=nb: e.matmul(
                                pp0.ap()[:, qt * 1024 + cb * 512: qt * 1024 + cb * 512 + 512],
                                PTs[:, ks * 128:ks * 128 + 128],
                                Vb[sl][:, ks * 1024 + cb * 512: ks * 1024 + cb * 512 + 512],
                                start=(b == 0 and ks == 0), stop=(b == nb - 1 and ks == 1)),
                                inc="sT" if (ks == 1 and cb == 1) else None)
                    pvmm_prev = mv
            pvdone = P.cnt("sT")
            if p5stage < 3:
                continue
            P.ins("vector", lambda e: e.drain())
            for qt in range(2):
                P.ins("vector", lambda e, qt=qt: e.reciprocal(
                    linv[:, qt:qt + 1], lrun[:, qt:qt + 1]))
            lnv5 = P.ins("vector", lambda e: e.drain(), inc="sV")
            for qt in range(2):
                P.wait("scalar", "sV", lnv5)
                P.wait("scalar", "sT", pvdone)
                onv = P.ins("scalar", lambda e, qt=qt: e.activation(
                    onb, pp0.ap()[:, qt * 1024:qt * 1024 + 1024], AF.Identity,
                    scale=linv[:, qt:qt + 1]), inc="sA")
                P.wait("tensor", "sA", onv)
                P.wait("tensor", "sA")
                P.wait("tensor", "sV")
                tv = None
                for cc in range(8):
                    tv = P.ins("tensor", lambda e, cc=cc: e.transpose(
                        pp1.ap()[:, cc * 128:cc * 128 + 128],
                        onb[:, cc * 128:cc * 128 + 128], idf_a),
                        inc="sT" if cc == 7 else None)
                P.wait("scalar", "sT", tv)
                P.ins("scalar", lambda e, qt=qt: e.activation(
                    Ofm.rearrange("p (cc t) -> p cc t", cc=8)[:, :, qt * 128:qt * 128 + 128],
                    pp1.ap()[:, 0:1024].rearrange("p (cc t) -> p cc t", cc=8),
                    AF.Copy), inc="sA")
            ofmv = P.cnt("sA")
            P.wait("tensor", "sA", ofmv)
            P.wait("tensor", "q:lw")
            aov = None
            for coutc in range(8):
                P.wait("tensor", "sA")
                mv = None
                for cin in range(8):
                    mv = P.ins("tensor", lambda e, coutc=coutc, cin=cin: e.matmul(
                        pp0.ap()[:, coutc * 256:coutc * 256 + 256],
                        wo_sb[:, cin * 1024 + coutc * 128: cin * 1024 + coutc * 128 + 128],
                        Ofm[:, cin * 256:cin * 256 + 256],
                        start=(cin == 0), stop=(cin == 7)),
                        inc="sT" if cin == 7 else None)
                P.wait("scalar", "sT", mv)
                aov = P.ins("scalar", lambda e, coutc=coutc: e.activation(
                    AOfm[:, coutc * 256:coutc * 256 + 256],
                    pp0.ap()[:, coutc * 256:coutc * 256 + 256],
                    AF.Identity, bias=bop_a[:, coutc:coutc + 1]), inc="sA")
            for qt in range(2):
                P.wait("sync", "sA", ofmv)
                lpt = P.dma("sync", lambda e, o0=o0, qt=qt: e.dma_start(
                    out=ptl[qt], in_=Pt[o0 + qt * 128:o0 + qt * 128 + 128, :]), "pl%d" % qt)
                P.wait("tensor", "sA", aov)
                P.wait("tensor", "sA")
                P.wait("tensor", "sV")
                tv = None
                for cc in range(8):
                    tv = P.ins("tensor", lambda e, cc=cc, qt=qt: e.transpose(
                        pp1.ap()[:, cc * 128:cc * 128 + 128],
                        AOfm[:, cc * 256 + qt * 128: cc * 256 + qt * 128 + 128], idf_a),
                        inc="sT" if cc == 7 else None)
                P.wait("vector", "sT", tv)
                P.wait("vector", "q:pl%d" % qt, lpt)
                ov = P.ins("vector", lambda e, qt=qt: e.tensor_tensor(
                    outb[qt], pp1.ap()[:, 0:1024], ptl[qt], op=ALU.add), inc="sV")
                P.wait("gpsimd", "sV", ov)
                P.dma("gpsimd", lambda e, o0=o0, qt=qt: e.dma_start(
                    out=out[o0 + qt * 128:o0 + qt * 128 + 128, :], in_=outb[qt]), "so%d" % qt)
        pf.pop()
        P.wait("gpsimd", "q:so0")
        P.wait("gpsimd", "q:so1")
        P.flush()
    return nc


def _tables():
    inv = (1.0 / THETA) ** (np.arange(0, C, 2, dtype=np.float64) / C)
    tpos = np.arange(T, dtype=np.float64)
    fr = tpos[:, None] * inv[None, :]
    cos = np.cos(fr); sin = np.sin(fr)
    cosx = np.repeat(cos, 2, axis=1)
    sgn = np.empty((T, C), dtype=np.float64)
    sgn[:, 0::2] = -np.repeat(sin, 1, axis=1)[:, :]
    sgn[:, 1::2] = np.repeat(sin, 1, axis=1)[:, :]
    ch = cosx.astype(np.float32)
    cl = (cosx - ch).astype(ml_dtypes.bfloat16)
    sh = sgn.astype(np.float32)
    slo = (sgn - sh).astype(ml_dtypes.bfloat16)
    return ch, cl, sh, slo


def _masks(rows):
    ms = np.zeros((16, 128, 512), dtype=np.float32)
    ii = np.arange(128)[:, None]
    jj = np.arange(512)[None, :]
    for i, r in enumerate(rows):
        Lb = ((r + 128 + 511) // 512) * 512 - 512
        ms[i] = np.where(r + ii >= Lb + jj, 0.0, NEG)
    return ms


def kernel(L0_attn_x, L0_attn_Wq, L0_attn_bq, L0_attn_Wk, L0_attn_bk,
           L0_attn_Wv, L0_attn_bv, L0_attn_Wo, L0_attn_bo,
           L0_ffn_w_gate, L0_ffn_w_up, L0_ffn_w_down):
    x = np.ascontiguousarray(np.asarray(L0_attn_x, dtype=np.float32))
    ch, cl, sh, slo = _tables()
    ident = np.eye(128, dtype=np.float32)
    ones2 = np.ones((128, 128), dtype=np.float32)
    ones1 = np.ones((1, 128), dtype=np.float32)
    wgT = np.ascontiguousarray(np.asarray(L0_ffn_w_gate, np.float32).T)
    wuT = np.ascontiguousarray(np.asarray(L0_ffn_w_up, np.float32).T)
    wdT = np.ascontiguousarray(np.asarray(L0_ffn_w_down, np.float32).T)

    in_maps = []
    row_maps = []
    for core in range(8):
        b, h = core // 2, core % 2
        J = J0 if h == 0 else J1
        rows = np.concatenate([np.arange(c * 256, c * 256 + 256) for c in J])
        row_maps.append((b, rows))
        qrows = rows[::128]
        m = {
            "xb": np.ascontiguousarray(x[b]),
            "xown": np.ascontiguousarray(x[b][rows]),
            "tkch": ch, "tkcl": cl, "tksh": sh, "tksl": slo,
            "tqch": np.ascontiguousarray(ch[rows]),
            "tqcl": np.ascontiguousarray(cl[rows]),
            "tqsh": np.ascontiguousarray(sh[rows]),
            "tqsl": np.ascontiguousarray(slo[rows]),
            "wq": np.ascontiguousarray(np.asarray(L0_attn_Wq, np.float32)),
            "wk": np.ascontiguousarray(np.asarray(L0_attn_Wk, np.float32)),
            "wv": np.ascontiguousarray(np.asarray(L0_attn_Wv, np.float32)),
            "wo": np.ascontiguousarray(np.asarray(L0_attn_Wo, np.float32)),
            "bqv": np.asarray(L0_attn_bq, np.float32).reshape(1, C),
            "bkv": np.asarray(L0_attn_bk, np.float32).reshape(1, C),
            "bvv": np.asarray(L0_attn_bv, np.float32).reshape(1, C),
            "bov": np.asarray(L0_attn_bo, np.float32).reshape(1, C),
            "bopv": np.ascontiguousarray(np.asarray(L0_attn_bo, np.float32).reshape(8, 128).T),
            "wgT": wgT, "wuT": wuT, "wdT": wdT,
            "maskseq": _masks(qrows),
            "identv": ident, "ones2d": ones2, "ones1r": ones1,
        }
        in_maps.append(m)

    nc = build_nc()
    res = run_bass_kernel_spmd(nc, in_maps, core_ids=list(range(8)))
    full = np.empty((B, T, C), dtype=np.float32)
    for core in range(8):
        b, rows = row_maps[core]
        full[b, rows] = res.results[core]["out"]
    return full




# version 05
# speedup vs baseline: 1.0503x; 1.0503x over previous
import numpy as np
import ml_dtypes
import concourse.bass as bass
import concourse.mybir as mybir
from concourse.bass_utils import run_bass_kernel_spmd

F32 = mybir.dt.float32
BF16 = mybir.dt.bfloat16
I32 = mybir.dt.int32
AF = mybir.ActivationFunctionType
ALU = mybir.AluOpType
AX = mybir.AxisListType

B, T, C, INNER = 4, 4096, 1024, 4096
OWN = 2048
NW = float(INNER * C)
EPS = 1e-5
NEG = -1e30
THETA = 10000.0

J0 = [0, 3, 4, 7, 8, 11, 12, 15]
J1 = [1, 2, 5, 6, 9, 10, 13, 14]

ENGS = ("sync", "vector", "scalar", "tensor", "gpsimd")


class Pool:
    def __init__(self, ap):
        self.ap = ap
        self.cols = ap.shape[1]
        self.off = 0
        self.stack = []

    def alloc(self, n):
        assert self.off + n <= self.cols, f"pool overflow {self.off}+{n}>{self.cols}"
        v = self.ap[:, self.off:self.off + n]
        self.off += n
        return v

    def push(self):
        self.stack.append(self.off)

    def pop(self):
        self.off = self.stack.pop()


class Prog:

    def __init__(self, nc, block):
        self.nc, self.block = nc, block
        self.ops = {e: [] for e in ENGS}
        self.sem = {}
        self.loadsems = []
        self.storesems = []
        for name in ("sV", "sA", "sT", "bS", "bG"):
            h = nc.semaphore(name)
            self.sem[name] = [h.__enter__(), 0]

    def dma(self, eng, fn, stream):
        name = "q:" + stream
        if name not in self.sem:
            h = self.nc.semaphore(name)
            self.sem[name] = [h.__enter__(), 0]
            (self.loadsems if eng == "sync" else self.storesems).append(name)
        return self.ins(eng, fn, inc=name)

    def raw(self, eng, fn):
        self.ops[eng].append(fn)

    def ins(self, eng, fn, inc=None):
        if inc is None:
            self.ops[eng].append(fn)
            return None
        s = self.sem[inc]
        step = 16 if inc.startswith("q:") else 1
        s[1] += step
        h = s[0]
        self.ops[eng].append(lambda e, fn=fn, h=h, step=step: fn(e).then_inc(h, step))
        return s[1]

    def wait(self, eng, semname, val=None):
        if semname not in self.sem:
            return
        s = self.sem[semname]
        v = s[1] if val is None else val
        if v > 0:
            self.ops[eng].append(lambda e, h=s[0], v=v: e.wait_ge(h, v))

    def cnt(self, semname):
        return self.sem[semname][1]

    def barrier(self):
        for nm in self.loadsems:
            self.wait("sync", nm)
        for nm in self.storesems:
            self.wait("gpsimd", nm)
        marks = {}
        for eng, s in (("sync", "bS"), ("vector", "sV"), ("scalar", "sA"),
                       ("tensor", "sT"), ("gpsimd", "bG")):
            marks[eng] = self.ins(eng, lambda e: e.drain(), inc=s)
        for eng in ENGS:
            for oeng, s in (("sync", "bS"), ("vector", "sV"), ("scalar", "sA"),
                            ("tensor", "sT"), ("gpsimd", "bG")):
                if oeng != eng:
                    self.wait(eng, s, marks[oeng])

    def flush(self):
        for eng in ENGS:
            fns = self.ops[eng]
            self.ops[eng] = []
            if not fns:
                continue
            dec = getattr(self.block, eng)

            def section(e, fns=fns):
                for f in fns:
                    f(e)
            dec(section)


def _rearr_pc(ap, p=128):
    return ap.rearrange("(n p) m -> p n m", p=p)


def build_nc(phases=5, p5stage=3):
    nc = bass.Bass("TRN2", target_bir_lowering=False)

    xb = nc.dram_tensor("xb", [T, C], F32, kind="ExternalInput").ap()
    xown = nc.dram_tensor("xown", [OWN, C], F32, kind="ExternalInput").ap()
    tkch = nc.dram_tensor("tkch", [T, C], F32, kind="ExternalInput").ap()
    tksh = nc.dram_tensor("tksh", [T, C], F32, kind="ExternalInput").ap()
    tqch = nc.dram_tensor("tqch", [OWN, C], F32, kind="ExternalInput").ap()
    tqsh = nc.dram_tensor("tqsh", [OWN, C], F32, kind="ExternalInput").ap()
    wq = nc.dram_tensor("wq", [C, C], F32, kind="ExternalInput").ap()
    wk = nc.dram_tensor("wk", [C, C], F32, kind="ExternalInput").ap()
    wv = nc.dram_tensor("wv", [C, C], F32, kind="ExternalInput").ap()
    wo = nc.dram_tensor("wo", [C, C], F32, kind="ExternalInput").ap()
    bqv = nc.dram_tensor("bqv", [1, C], F32, kind="ExternalInput").ap()
    bkv = nc.dram_tensor("bkv", [1, C], F32, kind="ExternalInput").ap()
    bvv = nc.dram_tensor("bvv", [1, C], F32, kind="ExternalInput").ap()
    bov = nc.dram_tensor("bov", [1, C], F32, kind="ExternalInput").ap()
    bopv = nc.dram_tensor("bopv", [128, 8], F32, kind="ExternalInput").ap()
    wgT = nc.dram_tensor("wgT", [C, INNER], F32, kind="ExternalInput").ap()
    wuT = nc.dram_tensor("wuT", [C, INNER], F32, kind="ExternalInput").ap()
    wdT = nc.dram_tensor("wdT", [INNER, C], F32, kind="ExternalInput").ap()
    maskseq = nc.dram_tensor("maskseq", [16, 128, 512], F32, kind="ExternalInput").ap()
    identv = nc.dram_tensor("identv", [128, 128], F32, kind="ExternalInput").ap()
    ones2d = nc.dram_tensor("ones2d", [128, 128], F32, kind="ExternalInput").ap()
    ones1r = nc.dram_tensor("ones1r", [1, 128], F32, kind="ExternalInput").ap()
    out = nc.dram_tensor("out", [OWN, C], F32, kind="ExternalOutput").ap()


    Ks = nc.dram_tensor("Ks", [C, T], F32).ap()
    Vs = nc.dram_tensor("Vs", [T, C], F32).ap()
    Qs = nc.dram_tensor("Qs", [C, OWN], F32).ap()
    XQs = nc.dram_tensor("XQs", [C, OWN], BF16).ap()
    Gq = nc.dram_tensor("Gq", [C, INNER], BF16).ap()
    Uq = nc.dram_tensor("Uq", [C, INNER], BF16).ap()
    Dq = nc.dram_tensor("Dq", [INNER, C], BF16).ap()
    Pt = nc.dram_tensor("Pt", [OWN, C], F32).ap()

    from contextlib import ExitStack
    es = ExitStack()
    with es:
        pf_t = es.enter_context(nc.sbuf_tensor("pf", [128, 28672], F32))
        ph_t = es.enter_context(nc.sbuf_tensor("ph", [128, 32768], BF16))
        pi_t = es.enter_context(nc.sbuf_tensor("pi", [128, 2048], I32))
        sm_t = es.enter_context(nc.sbuf_tensor("sm", [128, 96], F32))
        idf = es.enter_context(nc.sbuf_tensor("idf", [128, 128], F32))
        idh = es.enter_context(nc.sbuf_tensor("idh", [128, 128], BF16))
        onx = es.enter_context(nc.sbuf_tensor("onx", [128, 128], F32))
        onr = es.enter_context(nc.sbuf_tensor("onr", [1, 128], F32))
        bop = es.enter_context(nc.sbuf_tensor("bop", [128, 8], F32))
        pp0 = es.enter_context(nc.psum_tensor("pp0", [128, 2048], F32))
        pp1 = es.enter_context(nc.psum_tensor("pp1", [128, 1024], F32))
        pph = es.enter_context(nc.psum_tensor("pph", [128, 2048], BF16))
        block = es.enter_context(nc.Block())

        pf = Pool(pf_t.ap())
        ph = Pool(ph_t.ap())
        pi = Pool(pi_t.ap())
        sm = Pool(sm_t.ap())
        P = Prog(nc, block)

        idf_a, idh_a, onx_a, onr_a, bop_a = idf.ap(), idh.ap(), onx.ap(), onr.ap(), bop.ap()

        mwg = sm.alloc(1); mwu = sm.alloc(1); mwd = sm.alloc(1)
        ra_inv = sm.alloc(16)
        sgc = sm.alloc(16); suc = sm.alloc(16)

        P.dma("sync", lambda e: e.dma_start(out=idf_a[:], in_=identv), "lc")
        P.dma("sync", lambda e: e.dma_start(out=onx_a[:], in_=ones2d), "lc")
        P.dma("sync", lambda e: e.dma_start(out=onr_a[:], in_=ones1r), "lc")
        P.dma("sync", lambda e: e.dma_start(out=bop_a[:], in_=bopv), "lc")
        P.wait("vector", "q:lc")
        P.ins("vector", lambda e: e.tensor_copy(idh_a[:], idf_a[:]), inc="sV")

        pf.push(); ph.push()
        wr = [pf.alloc(4096), pf.alloc(4096)]
        wvq = pf.alloc(4096)
        wnn = pf.alloc(4096)
        acc = sm.alloc(1)
        tabs = sm.alloc(1)
        swb = sm.alloc(1)
        wqo = [ph.alloc(4096), ph.alloc(4096)]

        qw_lastuse = [None, None]
        qw_lastq = [None, None]

        def quant_weight(src, dst, n_pc, cols, mw_tile):
            src_r = _rearr_pc(src)
            dst_r = _rearr_pc(dst)
            P.wait("sync", "sV")
            P.wait("sync", "sA")
            P.wait("scalar", "sV")
            lastuse = qw_lastuse
            prevacc = None
            for pc in range(n_pc):
                sl = pc % 2
                if lastuse[sl] is not None:
                    P.wait("sync", "sA", lastuse[sl])
                ld = P.dma("sync", lambda e, pc=pc, sl=sl: e.dma_start(
                    out=wr[sl][:, :cols], in_=src_r[:, pc, :]), "wr%d" % sl)
                P.wait("scalar", "q:wr%d" % sl, ld)
                if prevacc is not None:
                    P.wait("scalar", "sV", prevacc)
                a = P.ins("scalar", lambda e, sl=sl: e.activation(
                    wvq[:, :cols], wr[sl][:, :cols], AF.Abs, accum_out=tabs), inc="sA")
                lastuse[sl] = a
                P.wait("vector", "sA", a)
                P.ins("vector", lambda e: e.drain())
                if pc == 0:
                    prevacc = P.ins("vector", lambda e: e.tensor_copy(acc, tabs), inc="sV")
                else:
                    prevacc = P.ins("vector", lambda e: e.tensor_tensor(
                        acc, acc, tabs, op=ALU.add), inc="sV")
            P.wait("tensor", "q:lc")
            P.wait("tensor", "sV", prevacc)
            tmk = P.ins("tensor", lambda e: e.matmul(
                pp0.ap()[:, 0:1], onx_a[:], acc, start=True, stop=True), inc="sT")
            P.wait("vector", "sT", tmk)
            P.ins("vector", lambda e: e.tensor_scalar(
                mw_tile, pp0.ap()[:, 0:1], 1.0 / NW, 1e-5, ALU.mult, ALU.max))
            P.ins("vector", lambda e: e.drain())
            P.ins("vector", lambda e: e.reciprocal(swb, mw_tile))
            swv = P.ins("vector", lambda e: e.drain(), inc="sV")
            lastq = qw_lastq
            for pc in range(n_pc):
                sl = pc % 2
                if lastuse[sl] is not None:
                    P.wait("sync", "sA", lastuse[sl])
                ld = P.dma("sync", lambda e, pc=pc, sl=sl: e.dma_start(
                    out=wr[sl][:, :cols], in_=src_r[:, pc, :]), "wr%d" % sl)
                P.wait("scalar", "q:wr%d" % sl, ld)
                P.wait("scalar", "sV", swv)
                P.wait("scalar", "sV")
                a = P.ins("scalar", lambda e, sl=sl: e.activation(
                    wvq[:, :cols], wr[sl][:, :cols], AF.Identity, scale=swb), inc="sA")
                lastuse[sl] = a
                P.wait("vector", "sA", a)
                P.ins("vector", lambda e: e.tensor_single_scalar(
                    wnn[:, :cols], wvq[:, :cols], -0.5, ALU.is_le))
                P.ins("vector", lambda e: e.drain())
                if lastq[sl] is not None:
                    P.wait("vector", "q:sw%d" % sl, lastq[sl])
                v = P.ins("vector", lambda e, sl=sl: e.scalar_tensor_tensor(
                    wqo[sl][:, :cols], wvq[:, :cols], 0.5, wnn[:, :cols],
                    ALU.is_ge, ALU.subtract), inc="sV")
                P.wait("gpsimd", "sV", v)
                lastq[sl] = P.dma("gpsimd", lambda e, pc=pc, sl=sl: e.dma_start(
                    out=dst_r[:, pc, :], in_=wqo[sl][:, :cols]), "sw%d" % sl)

        if phases >= 1:
            quant_weight(wgT, Gq, 8, 4096, mwg)
            quant_weight(wuT, Uq, 8, 4096, mwu)
            quant_weight(wdT, Dq, 32, 1024, mwd)
        pf.pop(); ph.pop()
        P.barrier()

        def emit_ln(xt, xnt, pjscr, vecs):
            rs = P.ins("vector", lambda e: e.reduce_sum(vecs["a"], xt, axis=AX.X), inc="sV")
            P.ins("vector", lambda e: e.drain())
            nm = P.ins("vector", lambda e: e.tensor_scalar_mul(
                vecs["b"], vecs["a"], -1.0 / C), inc="sV")
            P.wait("scalar", "sV", nm)
            sq = P.ins("scalar", lambda e: e.activation(
                pjscr, xt, AF.Square, bias=vecs["b"], scale=1.0,
                accum_out=vecs["c"]), inc="sA")
            P.wait("vector", "sA", sq)
            vv = P.ins("vector", lambda e: e.tensor_scalar(
                vecs["d"], vecs["c"], 1.0 / C, EPS, ALU.mult, ALU.add), inc="sV")
            P.wait("scalar", "sV", vv)
            sr = P.ins("scalar", lambda e: e.activation(vecs["e"], vecs["d"], AF.Sqrt), inc="sA")
            P.wait("vector", "sA", sr)
            P.ins("vector", lambda e: e.drain())
            P.ins("vector", lambda e: e.reciprocal(vecs["f"], vecs["e"]))
            P.ins("vector", lambda e: e.drain())
            bb = P.ins("vector", lambda e: e.tensor_tensor(
                vecs["g"], vecs["b"], vecs["f"], op=ALU.mult), inc="sV")
            P.wait("scalar", "sV", bb)
            return P.ins("scalar", lambda e: e.activation(
                xnt, xt, AF.Identity, scale=vecs["f"], bias=vecs["g"]), inc="sA")

        def emit_transpose(src, dst, ident, psview, wait_sem, wait_val, bf16=False):
            P.wait("tensor", wait_sem, wait_val)
            P.wait("tensor", "sA")
            tv = None
            for ccb in range(8):
                tv = P.ins("tensor", lambda e, ccb=ccb: e.transpose(
                    psview[:, ccb * 128:(ccb + 1) * 128],
                    src[:, ccb * 128:(ccb + 1) * 128], ident),
                    inc="sT" if ccb == 7 else None)
            P.wait("scalar", "sT", tv)
            return P.ins("scalar", lambda e: e.activation(dst, psview, AF.Copy), inc="sA")

        def emit_proj(xnf, wsb, brow, psview, wait_val):
            P.wait("tensor", "sA", wait_val)
            mmv = None
            for cb in range(2):
                for cin in range(8):
                    P.ins("tensor", lambda e, cb=cb, cin=cin: e.matmul(
                        psview[:, cb * 512:(cb + 1) * 512],
                        xnf[:, cin * 128:(cin + 1) * 128],
                        wsb[:, cin * 1024 + cb * 512: cin * 1024 + cb * 512 + 512],
                        start=(cin == 0), stop=False))
                mmv = P.ins("tensor", lambda e, cb=cb: e.matmul(
                    psview[:, cb * 512:(cb + 1) * 512], onr_a[:],
                    brow[:, cb * 512:(cb + 1) * 512], start=False, stop=True),
                    inc="sT" if cb == 1 else None)
            return mmv

        def emit_rope(pj, rp, ct, st, tswp, ready_val):
            P.wait("vector", "sA", ready_val)
            P.ins("vector", lambda e: e.tensor_tensor(rp, pj, ct, op=ALU.mult))
            P.ins("vector", lambda e: e.tensor_tensor(
                tswp[:, 0:1024:2], pj[:, 1:1024:2], st[:, 0:1024:2], op=ALU.mult))
            P.ins("vector", lambda e: e.tensor_tensor(
                tswp[:, 1:1024:2], pj[:, 0:1024:2], st[:, 1:1024:2], op=ALU.mult))
            P.ins("vector", lambda e: e.drain())
            return P.ins("vector", lambda e: e.tensor_tensor(
                rp, rp, tswp, op=ALU.add), inc="sV")

        pf.push(); ph.push()
        wk_sb = pf.alloc(8192)
        wv_sb = pf.alloc(8192)
        bkr = pf.alloc(1024).ap()[0:1, :] if False else None
        pf.pop()
        pf.push()
        wk_sb = pf.alloc(8192)
        wv_sb = pf.alloc(8192)
        xt = pf.alloc(1024)
        xnt = pf.alloc(1024)
        xnf = pf.alloc(1024)
        pj = pf.alloc(1024)
        rp = pf.alloc(1024)
        ct = pf.alloc(1024)
        stt = pf.alloc(1024)
        tlo_h = ph.alloc(1024)
        tswp = pf.alloc(1024)
        kfm = pf.alloc(1024)
        vecs = {k: sm.alloc(1) for k in "abcdefg"}
        br_k = es.enter_context(nc.sbuf_tensor("br_k", [1, 1024], F32)).ap()
        br_v = es.enter_context(nc.sbuf_tensor("br_v", [1, 1024], F32)).ap()
        br_q = es.enter_context(nc.sbuf_tensor("br_q", [1, 1024], F32)).ap()

        P.dma("sync", lambda e: e.dma_start(
            out=wk_sb.rearrange("p (cc n) -> p cc n", cc=8),
            in_=wk.rearrange("(cc p) n -> p cc n", p=128)), "lw")
        P.dma("sync", lambda e: e.dma_start(
            out=wv_sb.rearrange("p (cc n) -> p cc n", cc=8),
            in_=wv.rearrange("(cc p) n -> p cc n", p=128)), "lw")
        P.dma("sync", lambda e: e.dma_start(out=br_k, in_=bkv), "lw")
        P.dma("sync", lambda e: e.dma_start(out=br_v, in_=bvv), "lw")
        wload = P.cnt("q:lw")

        Ks_r = _rearr_pc(Ks)
        Vs_r = Vs.rearrange("(n p) c -> p n c", p=128)

        def prep_tile(t0, is_q, qi):
            xsrc = xown if is_q else xb
            th_c = tqch if is_q else tkch
            th_s = tqsh if is_q else tksh
            P.wait("sync", "sA")
            P.wait("sync", "sV")
            P.wait("sync", "sT")
            P.wait("scalar", "q:sk")
            P.wait("scalar", "q:sv")
            P.wait("scalar", "q:sq")
            P.wait("scalar", "q:sxq")
            lx = P.dma("sync", lambda e: e.dma_start(out=xt, in_=xsrc[t0:t0 + 128, :]), "xt")
            l1 = P.dma("sync", lambda e: e.dma_start(out=ct, in_=th_c[t0:t0 + 128, :]), "ct")
            l3 = P.dma("sync", lambda e: e.dma_start(out=stt, in_=th_s[t0:t0 + 128, :]), "stt")
            P.wait("vector", "q:ct", l1)
            P.wait("vector", "q:stt", l3)
            P.wait("vector", "q:xt", lx)
            lnv = emit_ln(xt, xnt, pj, vecs)
            xnfv = emit_transpose(xnt, xnf, idf_a, pp0.ap()[:, 0:1024], "sA", lnv)
            if not is_q:
                kq = emit_proj(xnf, wk_sb, br_k, pp1.ap()[:, 0:1024], xnfv)
                P.wait("scalar", "sT", kq)
                kev = P.ins("scalar", lambda e: e.activation(pj, pp1.ap()[:, 0:1024], AF.Copy), inc="sA")
                rpv = emit_rope(pj, rp, ct, stt, tswp, kev)
                ktv = emit_transpose(rp, kfm, idf_a, pp0.ap()[:, 1024:2048], "sV", rpv)
                P.wait("gpsimd", "sA", ktv)
                P.dma("gpsimd", lambda e, t0=t0: e.dma_start(
                    out=Ks_r[:, :, t0:t0 + 128],
                    in_=kfm.rearrange("p (cc t) -> p cc t", cc=8)), "sk")
                vq = emit_proj(xnf, wv_sb, br_v, pp1.ap()[:, 0:1024], kev)
                P.wait("scalar", "sT", vq)
                P.wait("scalar", "sV", rpv)
                vev = P.ins("scalar", lambda e: e.activation(pj, pp1.ap()[:, 0:1024], AF.Copy), inc="sA")
                P.wait("gpsimd", "sA", vev)
                P.dma("gpsimd", lambda e, t0=t0: e.dma_start(
                    out=Vs_r[:, t0 // 128, :], in_=pj), "sv")
            else:
                qq = emit_proj(xnf, wk_sb, br_q, pp1.ap()[:, 0:1024], xnfv)
                P.wait("scalar", "sT", qq)
                qev = P.ins("scalar", lambda e: e.activation(pj, pp1.ap()[:, 0:1024], AF.Copy), inc="sA")
                rpv = emit_rope(pj, rp, ct, stt, tswp, qev)
                qtv = emit_transpose(rp, kfm, idf_a, pp0.ap()[:, 1024:2048], "sV", rpv)
                P.wait("gpsimd", "sA", qtv)
                P.dma("gpsimd", lambda e, t0=t0: e.dma_start(
                    out=Qs.rearrange("(cc p) t -> p cc t", p=128)[:, :, t0:t0 + 128],
                    in_=kfm.rearrange("p (cc t) -> p cc t", cc=8)), "sq")
                P.ins("vector", lambda e: e.reduce_max(
                    vecs["a"], xnt, axis=AX.X, apply_absolute_value=True))
                P.ins("vector", lambda e: e.drain())
                P.ins("vector", lambda e: e.tensor_single_scalar(vecs["b"], vecs["a"], 1e-5, ALU.max))
                P.ins("vector", lambda e: e.drain())
                P.ins("vector", lambda e: e.reciprocal(vecs["c"], vecs["b"]))
                P.ins("vector", lambda e: e.drain())
                P.ins("vector", lambda e: e.tensor_scalar_mul(vecs["d"], vecs["c"], 127.0))
                P.ins("vector", lambda e, qi=qi: e.tensor_scalar_mul(
                    ra_inv[:, qi:qi + 1], vecs["b"], 1.0 / 127.0))
                P.ins("vector", lambda e: e.drain())
                iv = P.ins("vector", lambda e: e.tensor_scalar(
                    pi_t.ap()[:, 0:1024], xnt, vecs["d"], None, ALU.mult), inc="sV")
                P.wait("scalar", "sV", iv)
                rv = P.ins("scalar", lambda e: e.activation(
                    ct, pi_t.ap()[:, 0:1024], AF.Copy, bias=-1.0, scale=1.0), inc="sA")
                P.ins("vector", lambda e: e.drain())
                P.ins("vector", lambda e: e.scalar_tensor_tensor(
                    tswp, xnt, vecs["d"], pi_t.ap()[:, 0:1024], ALU.mult, ALU.subtract))
                P.wait("vector", "sA", rv)
                P.ins("vector", lambda e: e.drain())
                P.ins("vector", lambda e: e.scalar_tensor_tensor(
                    stt, tswp, 0.5, ct, ALU.is_ge, ALU.add))
                P.ins("vector", lambda e: e.drain())
                nqv = P.ins("vector", lambda e: e.scalar_tensor_tensor(
                    tlo_h, tswp, -0.5, stt, ALU.is_ge, ALU.add), inc="sV")
                P.wait("tensor", "sV", nqv)
                tv = None
                for ccb in range(8):
                    tv = P.ins("tensor", lambda e, ccb=ccb: e.transpose(
                        pph.ap()[:, ccb * 128:ccb * 128 + 128],
                        tlo_h[:, ccb * 128:ccb * 128 + 128], idh_a),
                        inc="sT" if ccb == 7 else None)
                P.wait("scalar", "sT", tv)
                xqe = P.ins("scalar", lambda e: e.activation(
                    ph_qt, pph.ap()[:, 0:1024], AF.Copy), inc="sA")
                P.wait("gpsimd", "sA", xqe)
                P.dma("gpsimd", lambda e, t0=t0: e.dma_start(
                    out=XQs.rearrange("(cc p) t -> p cc t", p=128)[:, :, t0:t0 + 128],
                    in_=ph_qt.rearrange("p (cc t) -> p cc t", cc=8)), "sxq")

        ph_qt = ph.alloc(1024)
        P.wait("tensor", "q:lw", wload)
        if phases >= 2:
            for i in range(32):
                prep_tile(128 * i, False, 0)
        P.barrier()

        P.dma("sync", lambda e: e.dma_start(
            out=wk_sb.rearrange("p (cc n) -> p cc n", cc=8),
            in_=wq.rearrange("(cc p) n -> p cc n", p=128)), "lw")
        P.dma("sync", lambda e: e.dma_start(out=br_q, in_=bqv), "lw")
        P.wait("tensor", "q:lw")
        if phases >= 3:
            for i in range(16):
                prep_tile(128 * i, True, i)
        pf.pop(); ph.pop()
        P.barrier()

        pf.push(); ph.push()
        gu_full = [pf.alloc(4096) for _ in range(4)]
        rA = pf.alloc(2048); rB = pf.alloc(2048); rC = pf.alloc(2048)
        gatebuf = pf.alloc(1024)
        poutb = [pf.alloc(1024), pf.alloc(1024)]
        xo = [pf.alloc(1024), pf.alloc(1024)]
        xqg = ph.alloc(4096)
        wgq_q = ph.alloc(8192)
        wuq_q = ph.alloc(8192)
        nqh = ph.alloc(2048)
        gu_im = ph.alloc(8192)
        wdq_s = [ph.alloc(1024), ph.alloc(1024)]
        v4 = {k: sm.alloc(1) for k in "hijk"}
        sdc = sm.alloc(2)

        P.ins("vector", lambda e: e.tensor_scalar(sgc, ra_inv, mwg, None, ALU.mult))
        P.ins("vector", lambda e: e.tensor_scalar(suc, ra_inv, mwu, None, ALU.mult), inc="sV")

        Gq_r = _rearr_pc(Gq)
        Uq_r = _rearr_pc(Uq)
        Dq_r = _rearr_pc(Dq)
        Pt_r = Pt.rearrange("(n p) c -> p n c", p=128)
        XQs_r = XQs.rearrange("(cc p) t -> p cc t", p=128)

        for g in (range(4) if phases >= 4 else []):
            P.wait("sync", "sT")
            lq = P.dma("sync", lambda e, g=g: e.dma_start(
                out=xqg.rearrange("p (cc t) -> p cc t", cc=8),
                in_=XQs_r[:, :, g * 512:(g + 1) * 512]), "xqg")
            guv = [None] * 4
            for iq in range(4):
                P.wait("sync", "sT")
                lw1 = P.dma("sync", lambda e, iq=iq: e.dma_start(
                    out=wgq_q.rearrange("p (cc n) -> p cc n", cc=8),
                    in_=Gq_r[:, :, iq * 1024:(iq + 1) * 1024]), "wg")
                lw2 = P.dma("sync", lambda e, iq=iq: e.dma_start(
                    out=wuq_q.rearrange("p (cc n) -> p cc n", cc=8),
                    in_=Uq_r[:, :, iq * 1024:(iq + 1) * 1024]), "wu")
                for k in range(4):
                    tc = 4 * g + k
                    P.wait("tensor", "q:wg", lw1)
                    P.wait("tensor", "q:wu", lw2)
                    if iq == 0 and k == 0:
                        P.wait("tensor", "q:xqg", lq)
                    P.wait("tensor", "sV")
                    P.wait("tensor", "sA")
                    mmv = None
                    for cin in range(8):
                        for half in range(2):
                            P.ins("tensor", lambda e, cin=cin, half=half, k=k: e.matmul(
                                pp0.ap()[:, half * 512:half * 512 + 512],
                                xqg[:, cin * 512 + k * 128: cin * 512 + k * 128 + 128],
                                wgq_q[:, cin * 1024 + half * 512: cin * 1024 + half * 512 + 512],
                                start=(cin == 0), stop=(cin == 7)))
                        for half in range(2):
                            mmv = P.ins("tensor", lambda e, cin=cin, half=half, k=k: e.matmul(
                                pp0.ap()[:, 1024 + half * 512: 1024 + half * 512 + 512],
                                xqg[:, cin * 512 + k * 128: cin * 512 + k * 128 + 128],
                                wuq_q[:, cin * 1024 + half * 512: cin * 1024 + half * 512 + 512],
                                start=(cin == 0), stop=(cin == 7)),
                                inc="sT" if (cin == 7 and half == 1) else None)
                    P.wait("scalar", "sT", mmv)
                    P.wait("scalar", "sV")
                    sg = P.ins("scalar", lambda e, tc=tc: e.activation(
                        gatebuf, pp0.ap()[:, 0:1024], AF.Silu,
                        scale=sgc[:, tc:tc + 1]), inc="sA")
                    P.wait("vector", "sA", sg)
                    guv[k] = P.ins("vector", lambda e, tc=tc, k=k, iq=iq: e.scalar_tensor_tensor(
                        gu_full[k][:, iq * 1024:(iq + 1) * 1024],
                        pp1.ap()[:, 0:1024] if False else pp0.ap()[:, 1024:2048],
                        suc[:, tc:tc + 1], gatebuf, ALU.mult, ALU.mult), inc="sV")
            for dg in range(2):
                for kk in range(2):
                    k = dg * 2 + kk
                    tc = 4 * g + k
                    P.wait("vector", "sV", guv[k])
                    P.ins("vector", lambda e, k=k: e.reduce_max(
                        v4["h"], gu_full[k], axis=AX.X, apply_absolute_value=True))
                    P.ins("vector", lambda e: e.drain())
                    P.ins("vector", lambda e: e.tensor_single_scalar(v4["i"], v4["h"], 1e-5, ALU.max))
                    P.ins("vector", lambda e: e.drain())
                    P.ins("vector", lambda e: e.reciprocal(v4["j"], v4["i"]))
                    P.ins("vector", lambda e: e.drain())
                    P.ins("vector", lambda e: e.tensor_scalar_mul(v4["k"], v4["j"], 127.0))
                    P.ins("vector", lambda e, kk=kk: e.tensor_scalar_mul(
                        sdc[:, kk:kk + 1], v4["i"], 1.0 / 127.0))
                    P.ins("vector", lambda e: e.drain())
                    P.ins("vector", lambda e, kk=kk: e.tensor_scalar(
                        sdc[:, kk:kk + 1], sdc[:, kk:kk + 1], mwd, None, ALU.mult))
                    for hh in range(2):
                        sl = slice(hh * 2048, hh * 2048 + 2048)
                        P.wait("vector", "sT")
                        iv = P.ins("vector", lambda e, k=k, sl=sl: e.tensor_scalar(
                            pi_t.ap()[:, 0:2048], gu_full[k][:, sl], v4["k"], None,
                            ALU.mult), inc="sV")
                        P.wait("scalar", "sV", iv)
                        rv = P.ins("scalar", lambda e: e.activation(
                            rA, pi_t.ap()[:, 0:2048], AF.Copy, bias=-1.0, scale=1.0), inc="sA")
                        P.ins("vector", lambda e: e.drain())
                        P.ins("vector", lambda e, k=k, sl=sl: e.scalar_tensor_tensor(
                            rB, gu_full[k][:, sl], v4["k"], pi_t.ap()[:, 0:2048],
                            ALU.mult, ALU.subtract))
                        P.wait("vector", "sA", rv)
                        P.ins("vector", lambda e: e.drain())
                        P.ins("vector", lambda e: e.scalar_tensor_tensor(
                            rC, rB, 0.5, rA, ALU.is_ge, ALU.add))
                        P.ins("vector", lambda e: e.drain())
                        nv = P.ins("vector", lambda e: e.scalar_tensor_tensor(
                            nqh, rB, -0.5, rC, ALU.is_ge, ALU.add), inc="sV")
                        P.wait("tensor", "sV", nv)
                        P.wait("tensor", "sA")
                        tv = None
                        for iic in range(16):
                            tv = P.ins("tensor", lambda e, iic=iic: e.transpose(
                                pph.ap()[:, iic * 128:iic * 128 + 128],
                                nqh[:, iic * 128:iic * 128 + 128], idh_a),
                                inc="sT" if iic == 15 else None)
                        P.wait("scalar", "sT", tv)
                        P.ins("scalar", lambda e, hh=hh, kk=kk: e.activation(
                            gu_im.rearrange("p (ic kv t) -> p ic kv t", ic=32, kv=2)
                            [:, hh * 16:hh * 16 + 16, kk, :],
                            pph.ap().rearrange("p (ic t) -> p ic t", ic=16),
                            AF.Copy), inc="sA")
                P.wait("sync", "sA")
                P.wait("sync", "sT")
                lxo = [None, None]
                for kk in range(2):
                    tc = 4 * g + dg * 2 + kk
                    lxo[kk] = P.dma("sync", lambda e, tc=tc, kk=kk: e.dma_start(
                        out=xo[kk], in_=xown[tc * 128:tc * 128 + 128, :]), "xo%d" % kk)
                gusA = P.cnt("sA")
                P.wait("tensor", "sA", gusA)
                mmd = None
                icmark = {}
                for ic in range(32):
                    sl2 = ic % 2
                    if ic >= 2:
                        P.wait("sync", "sT", icmark[ic - 1])
                    lw = P.dma("sync", lambda e, ic=ic, sl2=sl2: e.dma_start(
                        out=wdq_s[sl2], in_=Dq_r[:, ic, :]), "wd%d" % sl2)
                    P.wait("tensor", "q:wd%d" % sl2, lw)
                    for kk in range(2):
                        for cb in range(2):
                            mmd = P.ins("tensor", lambda e, ic=ic, kk=kk, cb=cb, sl2=sl2: e.matmul(
                                pp0.ap()[:, kk * 1024 + cb * 512: kk * 1024 + cb * 512 + 512],
                                gu_im[:, ic * 256 + kk * 128: ic * 256 + kk * 128 + 128],
                                wdq_s[sl2][:, cb * 512:cb * 512 + 512],
                                start=(ic == 0), stop=(ic == 31)),
                                inc="sT" if (kk == 1 and cb == 1) else None)
                    icmark[ic] = mmd
                P.wait("vector", "sT", mmd)
                P.wait("vector", "q:sp0")
                P.wait("vector", "q:sp1")
                for kk in range(2):
                    tc = 4 * g + dg * 2 + kk
                    P.wait("vector", "q:xo%d" % kk, lxo[kk])
                    pv = P.ins("vector", lambda e, kk=kk: e.scalar_tensor_tensor(
                        poutb[kk], pp0.ap()[:, kk * 1024:kk * 1024 + 1024],
                        sdc[:, kk:kk + 1], xo[kk], ALU.mult, ALU.add), inc="sV")
                    P.wait("gpsimd", "sV", pv)
                    P.dma("gpsimd", lambda e, tc=tc, kk=kk: e.dma_start(
                        out=Pt_r[:, tc, :], in_=poutb[kk]), "sp%d" % kk)
        pf.pop(); ph.pop()
        P.barrier()

        pf.push()
        S = [pf.alloc(4096), pf.alloc(4096)]
        Kb = [pf.alloc(2048), pf.alloc(2048)]
        Vb = [pf.alloc(2048), pf.alloc(2048)]
        Qp = pf.alloc(2048)
        PTs = pf.alloc(256)
        Pblk = [pf.alloc(256), pf.alloc(256)]
        msk = [pf.alloc(512), pf.alloc(512)]
        wo_sb = pf.alloc(8192)
        v5 = {k: sm.alloc(1) for k in "mnop"}
        nmv = sm.alloc(2)
        bias_e = sm.alloc(2)
        lrun = sm.alloc(2)
        lsb = sm.alloc(2)
        linv = sm.alloc(2)

        P.dma("sync", lambda e: e.dma_start(
            out=wo_sb.rearrange("p (cc n) -> p cc n", cc=8),
            in_=wo.rearrange("(cc p) n -> p cc n", p=128)), "lw")
        Vs_rd = Vs.rearrange("(n p) c -> p n c", p=128)

        AOfm = S[0][:, 0:2048]
        Ofm = S[1][:, 0:2048]
        onb = S[1][:, 2048:3072]
        ptl = [S[0][:, 2048:3072], S[0][:, 3072:4096]]
        outb = [Kb[0][:, 0:1024], Kb[1][:, 0:1024]]

        for p5 in (range(8) if phases >= 5 else []):
            L = 512 * (p5 + 1)
            nb = L // 256
            o0 = 256 * p5
            P.wait("sync", "sT")
            P.wait("sync", "q:so0")
            P.wait("sync", "q:so1")
            P.wait("scalar", "sV")
            lq = P.dma("sync", lambda e, o0=o0: e.dma_start(
                out=Qp.rearrange("p (cc t) -> p cc t", cc=8),
                in_=Qs.rearrange("(cc p) t -> p cc t", p=128)[:, :, o0:o0 + 256]), "qp")
            P.wait("sync", "sV")
            lm0 = P.dma("sync", lambda e, p5=p5: e.dma_start(
                out=msk[0], in_=maskseq[2 * p5, :, :]), "mk0")
            lm1 = P.dma("sync", lambda e, p5=p5: e.dma_start(
                out=msk[1], in_=maskseq[2 * p5 + 1, :, :]), "mk1")
            evs = [None, None]
            for b in range(nb):
                sl = b % 2
                P.wait("sync", "sT")
                lk = P.dma("sync", lambda e, b=b, sl=sl: e.dma_start(
                    out=Kb[sl].rearrange("p (cc t) -> p cc t", cc=8),
                    in_=Ks_r[:, :, b * 256:b * 256 + 256]), "kb%d" % sl)
                P.wait("tensor", "q:kb%d" % sl, lk)
                if b == 0:
                    P.wait("tensor", "q:qp", lq)
                for qt in range(2):
                    psl = (2 * b + qt) % 4
                    psS = pp0.ap()[:, psl * 512:psl * 512 + 256]
                    P.wait("tensor", "sA")
                    mv = None
                    for cc in range(8):
                        mv = P.ins("tensor", lambda e, cc=cc, qt=qt, sl=sl, psS=psS: e.matmul(
                            psS, Qp[:, cc * 256 + qt * 128: cc * 256 + qt * 128 + 128],
                            Kb[sl][:, cc * 256:cc * 256 + 256],
                            start=(cc == 0), stop=(cc == 7)),
                            inc="sT" if cc == 7 else None)
                    P.wait("scalar", "sT", mv)
                    evs[qt] = P.ins("scalar", lambda e, qt=qt, b=b, psS=psS: e.activation(
                        S[qt][:, b * 256:b * 256 + 256], psS, AF.Copy), inc="sA")
            for qt in range(2):
                P.wait("vector", "sA", evs[qt])
                P.wait("vector", "q:mk0", lm0)
                P.wait("vector", "q:mk1", lm1)
                P.ins("vector", lambda e, qt=qt, L=L: e.tensor_tensor(
                    S[qt][:, L - 512:L], S[qt][:, L - 512:L], msk[qt], op=ALU.add))
                P.ins("vector", lambda e: e.drain())
                P.ins("vector", lambda e, qt=qt, L=L: e.reduce_max(
                    nmv[:, qt:qt + 1], S[qt][:, 0:L], axis=AX.X, negate=True))
                P.ins("vector", lambda e: e.drain())
                P.ins("vector", lambda e, qt=qt: e.tensor_scalar_mul(
                    bias_e[:, qt:qt + 1], nmv[:, qt:qt + 1], 1.0 / 32.0), inc="sV")
            mbv = P.cnt("sV")
            if p5stage < 2:
                continue
            pte_prev = None
            pvmm_prev = None
            lsb_used = [None, None]
            for b in range(nb):
                sl = b % 2
                if pvmm_prev is not None:
                    P.wait("sync", "sT", pvmm_prev)
                lv = P.dma("sync", lambda e, b=b, sl=sl: e.dma_start(
                    out=Vb[sl].rearrange("p (n c) -> p n c", n=2),
                    in_=Vs_rd[:, 2 * b:2 * b + 2, :]), "vb%d" % sl)
                for qt in range(2):
                    P.wait("scalar", "sV", mbv)
                    if lsb_used[qt] is not None:
                        P.wait("scalar", "sV", lsb_used[qt])
                    P.wait("scalar", "sT")
                    ex = P.ins("scalar", lambda e, qt=qt, b=b: e.activation(
                        Pblk[qt], S[qt][:, b * 256:b * 256 + 256], AF.Exp,
                        bias=bias_e[:, qt:qt + 1], scale=1.0 / 32.0,
                        accum_out=lsb[:, qt:qt + 1]), inc="sA")
                    P.wait("vector", "sA", ex)
                    P.ins("vector", lambda e: e.drain())
                    if b == 0:
                        lsb_used[qt] = P.ins("vector", lambda e, qt=qt: e.tensor_copy(
                            lrun[:, qt:qt + 1], lsb[:, qt:qt + 1]), inc="sV")
                    else:
                        lsb_used[qt] = P.ins("vector", lambda e, qt=qt: e.tensor_tensor(
                            lrun[:, qt:qt + 1], lrun[:, qt:qt + 1], lsb[:, qt:qt + 1],
                            op=ALU.add), inc="sV")
                    P.wait("tensor", "sA", ex)
                    if pte_prev is not None:
                        P.wait("tensor", "sA", pte_prev)
                    tv = None
                    for ks in range(2):
                        tv = P.ins("tensor", lambda e, ks=ks, qt=qt: e.transpose(
                            pp1.ap()[:, ks * 128:ks * 128 + 128],
                            Pblk[qt][:, ks * 128:ks * 128 + 128], idf_a),
                            inc="sT" if ks == 1 else None)
                    P.wait("scalar", "sT", tv)
                    if pvmm_prev is not None:
                        P.wait("scalar", "sT", pvmm_prev)
                    pte_prev = P.ins("scalar", lambda e: e.activation(
                        PTs, pp1.ap()[:, 0:256], AF.Copy), inc="sA")
                    P.wait("tensor", "sA", pte_prev)
                    P.wait("tensor", "q:vb%d" % sl, lv)
                    mv = None
                    for ks in range(2):
                        for cb in range(2):
                            mv = P.ins("tensor", lambda e, ks=ks, cb=cb, qt=qt, sl=sl, b=b, nb=nb: e.matmul(
                                pp0.ap()[:, qt * 1024 + cb * 512: qt * 1024 + cb * 512 + 512],
                                PTs[:, ks * 128:ks * 128 + 128],
                                Vb[sl][:, ks * 1024 + cb * 512: ks * 1024 + cb * 512 + 512],
                                start=(b == 0 and ks == 0), stop=(b == nb - 1 and ks == 1)),
                                inc="sT" if (ks == 1 and cb == 1) else None)
                    pvmm_prev = mv
            pvdone = P.cnt("sT")
            if p5stage < 3:
                continue
            P.ins("vector", lambda e: e.drain())
            for qt in range(2):
                P.ins("vector", lambda e, qt=qt: e.reciprocal(
                    linv[:, qt:qt + 1], lrun[:, qt:qt + 1]))
            lnv5 = P.ins("vector", lambda e: e.drain(), inc="sV")
            for qt in range(2):
                P.wait("scalar", "sV", lnv5)
                P.wait("scalar", "sT", pvdone)
                onv = P.ins("scalar", lambda e, qt=qt: e.activation(
                    onb, pp0.ap()[:, qt * 1024:qt * 1024 + 1024], AF.Identity,
                    scale=linv[:, qt:qt + 1]), inc="sA")
                P.wait("tensor", "sA", onv)
                P.wait("tensor", "sA")
                P.wait("tensor", "sV")
                tv = None
                for cc in range(8):
                    tv = P.ins("tensor", lambda e, cc=cc: e.transpose(
                        pp1.ap()[:, cc * 128:cc * 128 + 128],
                        onb[:, cc * 128:cc * 128 + 128], idf_a),
                        inc="sT" if cc == 7 else None)
                P.wait("scalar", "sT", tv)
                P.ins("scalar", lambda e, qt=qt: e.activation(
                    Ofm.rearrange("p (cc t) -> p cc t", cc=8)[:, :, qt * 128:qt * 128 + 128],
                    pp1.ap()[:, 0:1024].rearrange("p (cc t) -> p cc t", cc=8),
                    AF.Copy), inc="sA")
            ofmv = P.cnt("sA")
            P.wait("tensor", "sA", ofmv)
            P.wait("tensor", "q:lw")
            aov = None
            for coutc in range(8):
                P.wait("tensor", "sA")
                mv = None
                for cin in range(8):
                    mv = P.ins("tensor", lambda e, coutc=coutc, cin=cin: e.matmul(
                        pp0.ap()[:, coutc * 256:coutc * 256 + 256],
                        wo_sb[:, cin * 1024 + coutc * 128: cin * 1024 + coutc * 128 + 128],
                        Ofm[:, cin * 256:cin * 256 + 256],
                        start=(cin == 0), stop=(cin == 7)),
                        inc="sT" if cin == 7 else None)
                P.wait("scalar", "sT", mv)
                aov = P.ins("scalar", lambda e, coutc=coutc: e.activation(
                    AOfm[:, coutc * 256:coutc * 256 + 256],
                    pp0.ap()[:, coutc * 256:coutc * 256 + 256],
                    AF.Identity, bias=bop_a[:, coutc:coutc + 1]), inc="sA")
            for qt in range(2):
                P.wait("sync", "sA", ofmv)
                lpt = P.dma("sync", lambda e, o0=o0, qt=qt: e.dma_start(
                    out=ptl[qt], in_=Pt[o0 + qt * 128:o0 + qt * 128 + 128, :]), "pl%d" % qt)
                P.wait("tensor", "sA", aov)
                P.wait("tensor", "sA")
                P.wait("tensor", "sV")
                tv = None
                for cc in range(8):
                    tv = P.ins("tensor", lambda e, cc=cc, qt=qt: e.transpose(
                        pp1.ap()[:, cc * 128:cc * 128 + 128],
                        AOfm[:, cc * 256 + qt * 128: cc * 256 + qt * 128 + 128], idf_a),
                        inc="sT" if cc == 7 else None)
                P.wait("vector", "sT", tv)
                P.wait("vector", "q:pl%d" % qt, lpt)
                ov = P.ins("vector", lambda e, qt=qt: e.tensor_tensor(
                    outb[qt], pp1.ap()[:, 0:1024], ptl[qt], op=ALU.add), inc="sV")
                P.wait("gpsimd", "sV", ov)
                P.dma("gpsimd", lambda e, o0=o0, qt=qt: e.dma_start(
                    out=out[o0 + qt * 128:o0 + qt * 128 + 128, :], in_=outb[qt]), "so%d" % qt)
        pf.pop()
        P.wait("gpsimd", "q:so0")
        P.wait("gpsimd", "q:so1")
        P.flush()
    return nc


def _tables():
    import jax
    import jax.numpy as jnp
    with jax.default_device(jax.devices("cpu")[0]):
        inv_freq = 1.0 / THETA ** (jnp.arange(0, C, 2, dtype=jnp.float32) / C)
        freqs = jnp.arange(T, dtype=jnp.float32)[:, None] * inv_freq[None, :]
        freqs = jnp.repeat(freqs, 2, axis=-1)
        ch = np.asarray(jnp.cos(freqs), dtype=np.float32)
        sin = np.asarray(jnp.sin(freqs), dtype=np.float32)
    ss = sin.copy()
    ss[:, 0::2] = -sin[:, 0::2]
    return np.ascontiguousarray(ch), np.ascontiguousarray(ss)


def _masks(rows):
    ms = np.zeros((16, 128, 512), dtype=np.float32)
    ii = np.arange(128)[:, None]
    jj = np.arange(512)[None, :]
    for i, r in enumerate(rows):
        Lb = ((r + 128 + 511) // 512) * 512 - 512
        ms[i] = np.where(r + ii >= Lb + jj, 0.0, NEG)
    return ms


def kernel(L0_attn_x, L0_attn_Wq, L0_attn_bq, L0_attn_Wk, L0_attn_bk,
           L0_attn_Wv, L0_attn_bv, L0_attn_Wo, L0_attn_bo,
           L0_ffn_w_gate, L0_ffn_w_up, L0_ffn_w_down):
    x = np.ascontiguousarray(np.asarray(L0_attn_x, dtype=np.float32))
    ch, ss = _tables()
    ident = np.eye(128, dtype=np.float32)
    ones2 = np.ones((128, 128), dtype=np.float32)
    ones1 = np.ones((1, 128), dtype=np.float32)
    wgT = np.ascontiguousarray(np.asarray(L0_ffn_w_gate, np.float32).T)
    wuT = np.ascontiguousarray(np.asarray(L0_ffn_w_up, np.float32).T)
    wdT = np.ascontiguousarray(np.asarray(L0_ffn_w_down, np.float32).T)

    in_maps = []
    row_maps = []
    for core in range(8):
        b, h = core // 2, core % 2
        J = J0 if h == 0 else J1
        rows = np.concatenate([np.arange(c * 256, c * 256 + 256) for c in J])
        row_maps.append((b, rows))
        qrows = rows[::128]
        m = {
            "xb": np.ascontiguousarray(x[b]),
            "xown": np.ascontiguousarray(x[b][rows]),
            "tkch": ch, "tksh": ss,
            "tqch": np.ascontiguousarray(ch[rows]),
            "tqsh": np.ascontiguousarray(ss[rows]),
            "wq": np.ascontiguousarray(np.asarray(L0_attn_Wq, np.float32)),
            "wk": np.ascontiguousarray(np.asarray(L0_attn_Wk, np.float32)),
            "wv": np.ascontiguousarray(np.asarray(L0_attn_Wv, np.float32)),
            "wo": np.ascontiguousarray(np.asarray(L0_attn_Wo, np.float32)),
            "bqv": np.asarray(L0_attn_bq, np.float32).reshape(1, C),
            "bkv": np.asarray(L0_attn_bk, np.float32).reshape(1, C),
            "bvv": np.asarray(L0_attn_bv, np.float32).reshape(1, C),
            "bov": np.asarray(L0_attn_bo, np.float32).reshape(1, C),
            "bopv": np.ascontiguousarray(np.asarray(L0_attn_bo, np.float32).reshape(8, 128).T),
            "wgT": wgT, "wuT": wuT, "wdT": wdT,
            "maskseq": _masks(qrows),
            "identv": ident, "ones2d": ones2, "ones1r": ones1,
        }
        in_maps.append(m)

    nc = build_nc()
    res = run_bass_kernel_spmd(nc, in_maps, core_ids=list(range(8)))
    full = np.empty((B, T, C), dtype=np.float32)
    for core in range(8):
        b, rows = row_maps[core]
        full[b, rows] = res.results[core]["out"]
    return full




# version 06
# speedup vs baseline: 1.0733x; 1.0220x over previous
import numpy as np
import ml_dtypes
import concourse.bass as bass
import concourse.mybir as mybir
from concourse.bass_utils import run_bass_kernel_spmd

F32 = mybir.dt.float32
BF16 = mybir.dt.bfloat16
I32 = mybir.dt.int32
AF = mybir.ActivationFunctionType
ALU = mybir.AluOpType
AX = mybir.AxisListType

B, T, C, INNER = 4, 4096, 1024, 4096
OWN = 2048
NW = float(INNER * C)
EPS = 1e-5
NEG = -1e30
THETA = 10000.0

J0 = [0, 3, 4, 7, 8, 11, 12, 15]
J1 = [1, 2, 5, 6, 9, 10, 13, 14]

ENGS = ("sync", "vector", "scalar", "tensor", "gpsimd")


class Pool:
    def __init__(self, ap):
        self.ap = ap
        self.cols = ap.shape[1]
        self.off = 0
        self.stack = []

    def alloc(self, n):
        assert self.off + n <= self.cols, f"pool overflow {self.off}+{n}>{self.cols}"
        v = self.ap[:, self.off:self.off + n]
        self.off += n
        return v

    def push(self):
        self.stack.append(self.off)

    def pop(self):
        self.off = self.stack.pop()


class Prog:

    def __init__(self, nc, block):
        self.nc, self.block = nc, block
        self.ops = {e: [] for e in ENGS}
        self.sem = {}
        self.dmasems = {}
        for name in ("sV", "sA", "sT", "bS", "bG"):
            h = nc.semaphore(name)
            self.sem[name] = [h.__enter__(), 0]

    def dma(self, eng, fn, stream):
        name = "q:" + stream
        if name not in self.sem:
            h = self.nc.semaphore(name)
            self.sem[name] = [h.__enter__(), 0]
            self.dmasems[name] = eng
        return self.ins(eng, fn, inc=name)

    def raw(self, eng, fn):
        self.ops[eng].append(fn)

    def ins(self, eng, fn, inc=None):
        if inc is None:
            self.ops[eng].append(fn)
            return None
        s = self.sem[inc]
        step = 16 if inc.startswith("q:") else 1
        s[1] += step
        h = s[0]
        self.ops[eng].append(lambda e, fn=fn, h=h, step=step: fn(e).then_inc(h, step))
        return s[1]

    def wait(self, eng, semname, val=None):
        if semname not in self.sem:
            return
        s = self.sem[semname]
        v = s[1] if val is None else val
        if v > 0:
            self.ops[eng].append(lambda e, h=s[0], v=v: e.wait_ge(h, v))

    def cnt(self, semname):
        return self.sem[semname][1]

    def barrier(self):
        for nm, eng in self.dmasems.items():
            self.wait(eng, nm)
        marks = {}
        for eng, s in (("sync", "bS"), ("vector", "sV"), ("scalar", "sA"),
                       ("tensor", "sT"), ("gpsimd", "bG")):
            marks[eng] = self.ins(eng, lambda e: e.drain(), inc=s)
        for eng in ENGS:
            for oeng, s in (("sync", "bS"), ("vector", "sV"), ("scalar", "sA"),
                            ("tensor", "sT"), ("gpsimd", "bG")):
                if oeng != eng:
                    self.wait(eng, s, marks[oeng])

    def flush(self):
        for eng in ENGS:
            fns = self.ops[eng]
            self.ops[eng] = []
            if not fns:
                continue
            dec = getattr(self.block, eng)

            def section(e, fns=fns):
                for f in fns:
                    f(e)
            dec(section)


def _rearr_pc(ap, p=128):
    return ap.rearrange("(n p) m -> p n m", p=p)


def build_nc(phases=5, p5stage=3):
    nc = bass.Bass("TRN2", target_bir_lowering=False)

    xb = nc.dram_tensor("xb", [T, C], F32, kind="ExternalInput").ap()
    xown = nc.dram_tensor("xown", [OWN, C], F32, kind="ExternalInput").ap()
    tkch = nc.dram_tensor("tkch", [T, C], F32, kind="ExternalInput").ap()
    tksh = nc.dram_tensor("tksh", [T, C], F32, kind="ExternalInput").ap()
    tqch = nc.dram_tensor("tqch", [OWN, C], F32, kind="ExternalInput").ap()
    tqsh = nc.dram_tensor("tqsh", [OWN, C], F32, kind="ExternalInput").ap()
    wq = nc.dram_tensor("wq", [C, C], F32, kind="ExternalInput").ap()
    wk = nc.dram_tensor("wk", [C, C], F32, kind="ExternalInput").ap()
    wv = nc.dram_tensor("wv", [C, C], F32, kind="ExternalInput").ap()
    wo = nc.dram_tensor("wo", [C, C], F32, kind="ExternalInput").ap()
    bqv = nc.dram_tensor("bqv", [1, C], F32, kind="ExternalInput").ap()
    bkv = nc.dram_tensor("bkv", [1, C], F32, kind="ExternalInput").ap()
    bvv = nc.dram_tensor("bvv", [1, C], F32, kind="ExternalInput").ap()
    bov = nc.dram_tensor("bov", [1, C], F32, kind="ExternalInput").ap()
    bopv = nc.dram_tensor("bopv", [128, 8], F32, kind="ExternalInput").ap()
    wgT = nc.dram_tensor("wgT", [C, INNER], F32, kind="ExternalInput").ap()
    wuT = nc.dram_tensor("wuT", [C, INNER], F32, kind="ExternalInput").ap()
    wdT = nc.dram_tensor("wdT", [INNER, C], F32, kind="ExternalInput").ap()
    maskseq = nc.dram_tensor("maskseq", [16, 128, 512], F32, kind="ExternalInput").ap()
    identv = nc.dram_tensor("identv", [128, 128], F32, kind="ExternalInput").ap()
    ones2d = nc.dram_tensor("ones2d", [128, 128], F32, kind="ExternalInput").ap()
    ones1r = nc.dram_tensor("ones1r", [1, 128], F32, kind="ExternalInput").ap()
    out = nc.dram_tensor("out", [OWN, C], F32, kind="ExternalOutput").ap()


    Ks = nc.dram_tensor("Ks", [C, T], F32).ap()
    Vs = nc.dram_tensor("Vs", [T, C], F32).ap()
    Qs = nc.dram_tensor("Qs", [C, OWN], F32).ap()
    XQs = nc.dram_tensor("XQs", [C, OWN], BF16).ap()
    Gq = nc.dram_tensor("Gq", [C, INNER], BF16).ap()
    Uq = nc.dram_tensor("Uq", [C, INNER], BF16).ap()
    Dq = nc.dram_tensor("Dq", [INNER, C], BF16).ap()
    Pt = nc.dram_tensor("Pt", [OWN, C], F32).ap()

    from contextlib import ExitStack
    es = ExitStack()
    with es:
        pf_t = es.enter_context(nc.sbuf_tensor("pf", [128, 29696], F32))
        ph_t = es.enter_context(nc.sbuf_tensor("ph", [128, 32768], BF16))
        pi_t = es.enter_context(nc.sbuf_tensor("pi", [128, 2048], I32))
        sm_t = es.enter_context(nc.sbuf_tensor("sm", [128, 96], F32))
        idf = es.enter_context(nc.sbuf_tensor("idf", [128, 128], F32))
        idh = es.enter_context(nc.sbuf_tensor("idh", [128, 128], BF16))
        onx = es.enter_context(nc.sbuf_tensor("onx", [128, 128], F32))
        onr = es.enter_context(nc.sbuf_tensor("onr", [1, 128], F32))
        bop = es.enter_context(nc.sbuf_tensor("bop", [128, 8], F32))
        pp0 = es.enter_context(nc.psum_tensor("pp0", [128, 2048], F32))
        pp1 = es.enter_context(nc.psum_tensor("pp1", [128, 1024], F32))
        pp2 = es.enter_context(nc.psum_tensor("pp2", [128, 1024], F32))
        block = es.enter_context(nc.Block())

        pf = Pool(pf_t.ap())
        ph = Pool(ph_t.ap())
        pi = Pool(pi_t.ap())
        sm = Pool(sm_t.ap())
        P = Prog(nc, block)

        idf_a, idh_a, onx_a, onr_a, bop_a = idf.ap(), idh.ap(), onx.ap(), onr.ap(), bop.ap()

        mwg = sm.alloc(1); mwu = sm.alloc(1); mwd = sm.alloc(1)
        ra_inv = sm.alloc(16)
        sgc = sm.alloc(16); suc = sm.alloc(16)

        P.dma("sync", lambda e: e.dma_start(out=idf_a[:], in_=identv), "lc")
        P.dma("sync", lambda e: e.dma_start(out=onx_a[:], in_=ones2d), "lc")
        P.dma("sync", lambda e: e.dma_start(out=onr_a[:], in_=ones1r), "lc")
        P.dma("sync", lambda e: e.dma_start(out=bop_a[:], in_=bopv), "lc")
        P.wait("vector", "q:lc")
        P.ins("vector", lambda e: e.tensor_copy(idh_a[:], idf_a[:]), inc="sV")

        pf.push(); ph.push()
        wr = [pf.alloc(4096), pf.alloc(4096)]
        wvq = pf.alloc(4096)
        wnn = pf.alloc(4096)
        acc = sm.alloc(1)
        tabs = sm.alloc(1)
        swb = sm.alloc(1)
        wqo = [ph.alloc(4096), ph.alloc(4096)]

        qw_lastuse = [None, None]
        qw_lastq = [None, None]

        def quant_weight(src, dst, n_pc, cols, mw_tile):
            src_r = _rearr_pc(src)
            dst_r = _rearr_pc(dst)
            P.wait("sync", "sV")
            P.wait("sync", "sA")
            P.wait("scalar", "sV")
            lastuse = qw_lastuse
            prevacc = None
            for pc in range(n_pc):
                sl = pc % 2
                ldeng = "sync" if sl == 0 else "gpsimd"
                if lastuse[sl] is not None:
                    P.wait(ldeng, "sA", lastuse[sl])
                ld = P.dma(ldeng, lambda e, pc=pc, sl=sl: e.dma_start(
                    out=wr[sl][:, :cols], in_=src_r[:, pc, :]), "wr%d" % sl)
                P.wait("scalar", "q:wr%d" % sl, ld)
                if prevacc is not None:
                    P.wait("scalar", "sV", prevacc)
                a = P.ins("scalar", lambda e, sl=sl: e.activation(
                    wvq[:, :cols], wr[sl][:, :cols], AF.Abs, accum_out=tabs), inc="sA")
                lastuse[sl] = a
                P.wait("vector", "sA", a)
                P.ins("vector", lambda e: e.drain())
                if pc == 0:
                    prevacc = P.ins("vector", lambda e: e.tensor_copy(acc, tabs), inc="sV")
                else:
                    prevacc = P.ins("vector", lambda e: e.tensor_tensor(
                        acc, acc, tabs, op=ALU.add), inc="sV")
            P.wait("tensor", "q:lc")
            P.wait("tensor", "sV", prevacc)
            tmk = P.ins("tensor", lambda e: e.matmul(
                pp0.ap()[:, 0:1], onx_a[:], acc, start=True, stop=True), inc="sT")
            P.wait("vector", "sT", tmk)
            P.ins("vector", lambda e: e.tensor_scalar(
                mw_tile, pp0.ap()[:, 0:1], 1.0 / NW, 1e-5, ALU.mult, ALU.max))
            P.ins("vector", lambda e: e.drain())
            P.ins("vector", lambda e: e.reciprocal(swb, mw_tile))
            swv = P.ins("vector", lambda e: e.drain(), inc="sV")
            lastq = qw_lastq
            for pc in range(n_pc):
                sl = pc % 2
                ldeng = "sync" if sl == 0 else "gpsimd"
                if lastuse[sl] is not None:
                    P.wait(ldeng, "sA", lastuse[sl])
                ld = P.dma(ldeng, lambda e, pc=pc, sl=sl: e.dma_start(
                    out=wr[sl][:, :cols], in_=src_r[:, pc, :]), "wr%d" % sl)
                P.wait("scalar", "q:wr%d" % sl, ld)
                P.wait("scalar", "sV", swv)
                P.wait("scalar", "sV")
                a = P.ins("scalar", lambda e, sl=sl: e.activation(
                    wvq[:, :cols], wr[sl][:, :cols], AF.Identity, scale=swb), inc="sA")
                lastuse[sl] = a
                P.wait("vector", "sA", a)
                P.ins("vector", lambda e: e.tensor_single_scalar(
                    wnn[:, :cols], wvq[:, :cols], -0.5, ALU.is_le))
                P.ins("vector", lambda e: e.drain())
                if lastq[sl] is not None:
                    P.wait("vector", "q:sw%d" % sl, lastq[sl])
                v = P.ins("vector", lambda e, sl=sl: e.scalar_tensor_tensor(
                    wqo[sl][:, :cols], wvq[:, :cols], 0.5, wnn[:, :cols],
                    ALU.is_ge, ALU.subtract), inc="sV")
                P.wait("gpsimd", "sV", v)
                lastq[sl] = P.dma("gpsimd", lambda e, pc=pc, sl=sl: e.dma_start(
                    out=dst_r[:, pc, :], in_=wqo[sl][:, :cols]), "sw%d" % sl)

        if phases >= 1:
            quant_weight(wgT, Gq, 8, 4096, mwg)
            quant_weight(wuT, Uq, 8, 4096, mwu)
            quant_weight(wdT, Dq, 32, 1024, mwd)
        pf.pop(); ph.pop()
        P.barrier()

        def emit_ln(xt, xnt, pjscr, vecs):
            rs = P.ins("vector", lambda e: e.reduce_sum(vecs["a"], xt, axis=AX.X), inc="sV")
            P.ins("vector", lambda e: e.drain())
            nm = P.ins("vector", lambda e: e.tensor_scalar_mul(
                vecs["b"], vecs["a"], -1.0 / C), inc="sV")
            P.wait("scalar", "sV", nm)
            sq = P.ins("scalar", lambda e: e.activation(
                pjscr, xt, AF.Square, bias=vecs["b"], scale=1.0,
                accum_out=vecs["c"]), inc="sA")
            P.wait("vector", "sA", sq)
            vv = P.ins("vector", lambda e: e.tensor_scalar(
                vecs["d"], vecs["c"], 1.0 / C, EPS, ALU.mult, ALU.add), inc="sV")
            P.wait("scalar", "sV", vv)
            sr = P.ins("scalar", lambda e: e.activation(vecs["e"], vecs["d"], AF.Sqrt), inc="sA")
            P.wait("vector", "sA", sr)
            P.ins("vector", lambda e: e.drain())
            P.ins("vector", lambda e: e.reciprocal(vecs["f"], vecs["e"]))
            P.ins("vector", lambda e: e.drain())
            bb = P.ins("vector", lambda e: e.tensor_tensor(
                vecs["g"], vecs["b"], vecs["f"], op=ALU.mult), inc="sV")
            P.wait("scalar", "sV", bb)
            return P.ins("scalar", lambda e: e.activation(
                xnt, xt, AF.Identity, scale=vecs["f"], bias=vecs["g"]), inc="sA")

        def emit_transpose(src, dst, ident, psview, wait_sem, wait_val, bf16=False):
            P.wait("tensor", wait_sem, wait_val)
            P.wait("tensor", "sA")
            tv = None
            for ccb in range(8):
                tv = P.ins("tensor", lambda e, ccb=ccb: e.transpose(
                    psview[:, ccb * 128:(ccb + 1) * 128],
                    src[:, ccb * 128:(ccb + 1) * 128], ident),
                    inc="sT" if ccb == 7 else None)
            P.wait("scalar", "sT", tv)
            return P.ins("scalar", lambda e: e.activation(dst, psview, AF.Copy), inc="sA")

        def emit_proj(xnf, wsb, brow, psview, wait_val):
            P.wait("tensor", "sA", wait_val)
            mmv = None
            for cb in range(2):
                for cin in range(8):
                    P.ins("tensor", lambda e, cb=cb, cin=cin: e.matmul(
                        psview[:, cb * 512:(cb + 1) * 512],
                        xnf[:, cin * 128:(cin + 1) * 128],
                        wsb[:, cin * 1024 + cb * 512: cin * 1024 + cb * 512 + 512],
                        start=(cin == 0), stop=False))
                mmv = P.ins("tensor", lambda e, cb=cb: e.matmul(
                    psview[:, cb * 512:(cb + 1) * 512], onr_a[:],
                    brow[:, cb * 512:(cb + 1) * 512], start=False, stop=True),
                    inc="sT" if cb == 1 else None)
            return mmv

        def emit_rope(pj, rp, ct, st, tswp, ready_val):
            P.wait("vector", "sA", ready_val)
            P.ins("vector", lambda e: e.tensor_tensor(rp, pj, ct, op=ALU.mult))
            P.ins("vector", lambda e: e.tensor_tensor(
                tswp[:, 0:1024:2], pj[:, 1:1024:2], st[:, 0:1024:2], op=ALU.mult))
            P.ins("vector", lambda e: e.tensor_tensor(
                tswp[:, 1:1024:2], pj[:, 0:1024:2], st[:, 1:1024:2], op=ALU.mult))
            P.ins("vector", lambda e: e.drain())
            return P.ins("vector", lambda e: e.tensor_tensor(
                rp, rp, tswp, op=ALU.add), inc="sV")

        pf.push(); ph.push()
        wk_sb = pf.alloc(8192)
        wv_sb = pf.alloc(8192)
        bkr = pf.alloc(1024).ap()[0:1, :] if False else None
        pf.pop()
        pf.push()
        wk_sb = pf.alloc(8192)
        wv_sb = pf.alloc(8192)
        xt = pf.alloc(1024)
        xnt = pf.alloc(1024)
        xnf = pf.alloc(1024)
        pj = pf.alloc(1024)
        rp = pf.alloc(1024)
        ct = pf.alloc(1024)
        stt = pf.alloc(1024)
        tswp = pf.alloc(1024)
        kfm = pf.alloc(1024)
        nqf = pf.alloc(1024)
        vecs = {k: sm.alloc(1) for k in "abcdefg"}
        br_k = es.enter_context(nc.sbuf_tensor("br_k", [1, 1024], F32)).ap()
        br_v = es.enter_context(nc.sbuf_tensor("br_v", [1, 1024], F32)).ap()
        br_q = es.enter_context(nc.sbuf_tensor("br_q", [1, 1024], F32)).ap()

        P.dma("sync", lambda e: e.dma_start(
            out=wk_sb.rearrange("p (cc n) -> p cc n", cc=8),
            in_=wk.rearrange("(cc p) n -> p cc n", p=128)), "lw")
        P.dma("sync", lambda e: e.dma_start(
            out=wv_sb.rearrange("p (cc n) -> p cc n", cc=8),
            in_=wv.rearrange("(cc p) n -> p cc n", p=128)), "lw")
        P.dma("sync", lambda e: e.dma_start(out=br_k, in_=bkv), "lw")
        P.dma("sync", lambda e: e.dma_start(out=br_v, in_=bvv), "lw")
        wload = P.cnt("q:lw")

        Ks_r = _rearr_pc(Ks)
        Vs_r = Vs.rearrange("(n p) c -> p n c", p=128)

        def prep_tile(t0, is_q, qi):
            xsrc = xown if is_q else xb
            th_c = tqch if is_q else tkch
            th_s = tqsh if is_q else tksh
            P.wait("sync", "sA")
            P.wait("sync", "sV")
            P.wait("sync", "sT")
            P.wait("scalar", "q:sk")
            P.wait("scalar", "q:sv")
            P.wait("scalar", "q:sq")
            P.wait("scalar", "q:sxq")
            lx = P.dma("sync", lambda e: e.dma_start(out=xt, in_=xsrc[t0:t0 + 128, :]), "xt")
            l1 = P.dma("sync", lambda e: e.dma_start(out=ct, in_=th_c[t0:t0 + 128, :]), "ct")
            l3 = P.dma("sync", lambda e: e.dma_start(out=stt, in_=th_s[t0:t0 + 128, :]), "stt")
            P.wait("vector", "q:ct", l1)
            P.wait("vector", "q:stt", l3)
            P.wait("vector", "q:xt", lx)
            lnv = emit_ln(xt, xnt, pj, vecs)
            xnfv = emit_transpose(xnt, xnf, idf_a, pp0.ap()[:, 0:1024], "sA", lnv)
            if not is_q:
                kq = emit_proj(xnf, wk_sb, br_k, pp1.ap()[:, 0:1024], xnfv)
                vq = emit_proj(xnf, wv_sb, br_v, pp2.ap()[:, 0:1024], xnfv)
                P.wait("scalar", "sT", kq)
                kev = P.ins("scalar", lambda e: e.activation(pj, pp1.ap()[:, 0:1024], AF.Copy), inc="sA")
                rpv = emit_rope(pj, rp, ct, stt, tswp, kev)
                ktv = emit_transpose(rp, kfm, idf_a, pp0.ap()[:, 1024:2048], "sV", rpv)
                P.wait("gpsimd", "sA", ktv)
                P.dma("gpsimd", lambda e, t0=t0: e.dma_start(
                    out=Ks_r[:, :, t0:t0 + 128],
                    in_=kfm.rearrange("p (cc t) -> p cc t", cc=8)), "sk")
                P.wait("scalar", "sT", vq)
                P.wait("scalar", "sV", rpv)
                vev = P.ins("scalar", lambda e: e.activation(pj, pp2.ap()[:, 0:1024], AF.Copy), inc="sA")
                P.wait("gpsimd", "sA", vev)
                P.dma("gpsimd", lambda e, t0=t0: e.dma_start(
                    out=Vs_r[:, t0 // 128, :], in_=pj), "sv")
            else:
                qq = emit_proj(xnf, wk_sb, br_q, pp1.ap()[:, 0:1024], xnfv)
                P.wait("scalar", "sT", qq)
                qev = P.ins("scalar", lambda e: e.activation(pj, pp1.ap()[:, 0:1024], AF.Copy), inc="sA")
                rpv = emit_rope(pj, rp, ct, stt, tswp, qev)
                qtv = emit_transpose(rp, kfm, idf_a, pp0.ap()[:, 1024:2048], "sV", rpv)
                P.wait("gpsimd", "sA", qtv)
                P.dma("gpsimd", lambda e, t0=t0: e.dma_start(
                    out=Qs.rearrange("(cc p) t -> p cc t", p=128)[:, :, t0:t0 + 128],
                    in_=kfm.rearrange("p (cc t) -> p cc t", cc=8)), "sq")
                P.ins("vector", lambda e: e.reduce_max(
                    vecs["a"], xnt, axis=AX.X, apply_absolute_value=True))
                P.ins("vector", lambda e: e.drain())
                P.ins("vector", lambda e: e.tensor_single_scalar(vecs["b"], vecs["a"], 1e-5, ALU.max))
                P.ins("vector", lambda e: e.drain())
                P.ins("vector", lambda e: e.reciprocal(vecs["c"], vecs["b"]))
                P.ins("vector", lambda e: e.drain())
                P.ins("vector", lambda e: e.tensor_scalar_mul(vecs["d"], vecs["c"], 127.0))
                P.ins("vector", lambda e, qi=qi: e.tensor_scalar_mul(
                    ra_inv[:, qi:qi + 1], vecs["b"], 1.0 / 127.0))
                P.ins("vector", lambda e: e.drain())
                iv = P.ins("vector", lambda e: e.tensor_scalar(
                    pi_t.ap()[:, 0:1024], xnt, vecs["d"], None, ALU.mult), inc="sV")
                P.wait("scalar", "sV", iv)
                rv = P.ins("scalar", lambda e: e.activation(
                    ct, pi_t.ap()[:, 0:1024], AF.Copy, bias=-1.0, scale=1.0), inc="sA")
                P.ins("vector", lambda e: e.drain())
                P.ins("vector", lambda e: e.scalar_tensor_tensor(
                    tswp, xnt, vecs["d"], pi_t.ap()[:, 0:1024], ALU.mult, ALU.subtract))
                P.wait("vector", "sA", rv)
                P.ins("vector", lambda e: e.drain())
                P.ins("vector", lambda e: e.scalar_tensor_tensor(
                    stt, tswp, 0.5, ct, ALU.is_ge, ALU.add))
                P.ins("vector", lambda e: e.drain())
                nqv = P.ins("vector", lambda e: e.scalar_tensor_tensor(
                    nqf, tswp, -0.5, stt, ALU.is_ge, ALU.add), inc="sV")
                P.wait("tensor", "sV", nqv)
                P.wait("tensor", "sA")
                tv = None
                for ccb in range(8):
                    tv = P.ins("tensor", lambda e, ccb=ccb: e.transpose(
                        pp0.ap()[:, ccb * 128:ccb * 128 + 128],
                        nqf[:, ccb * 128:ccb * 128 + 128], idf_a),
                        inc="sT" if ccb == 7 else None)
                P.wait("scalar", "sT", tv)
                xqe = P.ins("scalar", lambda e: e.activation(
                    ph_qt, pp0.ap()[:, 0:1024], AF.Copy), inc="sA")
                P.wait("gpsimd", "sA", xqe)
                P.dma("gpsimd", lambda e, t0=t0: e.dma_start(
                    out=XQs.rearrange("(cc p) t -> p cc t", p=128)[:, :, t0:t0 + 128],
                    in_=ph_qt.rearrange("p (cc t) -> p cc t", cc=8)), "sxq")

        ph_qt = ph.alloc(1024)
        P.wait("tensor", "q:lw", wload)
        if phases >= 2:
            for i in range(32):
                prep_tile(128 * i, False, 0)
        P.barrier()

        P.dma("sync", lambda e: e.dma_start(
            out=wk_sb.rearrange("p (cc n) -> p cc n", cc=8),
            in_=wq.rearrange("(cc p) n -> p cc n", p=128)), "lw")
        P.dma("sync", lambda e: e.dma_start(out=br_q, in_=bqv), "lw")
        P.wait("tensor", "q:lw")
        if phases >= 3:
            for i in range(16):
                prep_tile(128 * i, True, i)
        pf.pop(); ph.pop()
        P.barrier()

        pf.push(); ph.push()
        gu_full = [pf.alloc(4096) for _ in range(4)]
        rA = pf.alloc(2048); rB = pf.alloc(2048); rC = pf.alloc(2048)
        gatebuf = pf.alloc(1024)
        poutb = [pf.alloc(1024), pf.alloc(1024)]
        xo = [pf.alloc(1024), pf.alloc(1024)]
        xqg = ph.alloc(4096)
        wgq_q = ph.alloc(8192)
        wuq_q = ph.alloc(8192)
        nqf4 = pf.alloc(2048)
        gu_im = ph.alloc(8192)
        wdq_s = [ph.alloc(1024), ph.alloc(1024)]
        v4 = {k: sm.alloc(1) for k in "hijk"}
        sdc = sm.alloc(2)

        P.ins("vector", lambda e: e.tensor_scalar(sgc, ra_inv, mwg, None, ALU.mult))
        P.ins("vector", lambda e: e.tensor_scalar(suc, ra_inv, mwu, None, ALU.mult), inc="sV")

        Gq_r = _rearr_pc(Gq)
        Uq_r = _rearr_pc(Uq)
        Dq_r = _rearr_pc(Dq)
        Pt_r = Pt.rearrange("(n p) c -> p n c", p=128)
        XQs_r = XQs.rearrange("(cc p) t -> p cc t", p=128)

        for g in (range(4) if phases >= 4 else []):
            P.wait("sync", "sT")
            lq = P.dma("sync", lambda e, g=g: e.dma_start(
                out=xqg.rearrange("p (cc t) -> p cc t", cc=8),
                in_=XQs_r[:, :, g * 512:(g + 1) * 512]), "xqg")
            guv = [None] * 4
            for iq in range(4):
                P.wait("sync", "sT")
                lw1 = P.dma("sync", lambda e, iq=iq: e.dma_start(
                    out=wgq_q.rearrange("p (cc n) -> p cc n", cc=8),
                    in_=Gq_r[:, :, iq * 1024:(iq + 1) * 1024]), "wg")
                lw2 = P.dma("sync", lambda e, iq=iq: e.dma_start(
                    out=wuq_q.rearrange("p (cc n) -> p cc n", cc=8),
                    in_=Uq_r[:, :, iq * 1024:(iq + 1) * 1024]), "wu")
                for k in range(4):
                    tc = 4 * g + k
                    P.wait("tensor", "q:wg", lw1)
                    P.wait("tensor", "q:wu", lw2)
                    if iq == 0 and k == 0:
                        P.wait("tensor", "q:xqg", lq)
                    P.wait("tensor", "sV")
                    P.wait("tensor", "sA")
                    mmv = None
                    for cin in range(8):
                        for half in range(2):
                            P.ins("tensor", lambda e, cin=cin, half=half, k=k: e.matmul(
                                pp0.ap()[:, half * 512:half * 512 + 512],
                                xqg[:, cin * 512 + k * 128: cin * 512 + k * 128 + 128],
                                wgq_q[:, cin * 1024 + half * 512: cin * 1024 + half * 512 + 512],
                                start=(cin == 0), stop=(cin == 7)))
                        for half in range(2):
                            mmv = P.ins("tensor", lambda e, cin=cin, half=half, k=k: e.matmul(
                                pp0.ap()[:, 1024 + half * 512: 1024 + half * 512 + 512],
                                xqg[:, cin * 512 + k * 128: cin * 512 + k * 128 + 128],
                                wuq_q[:, cin * 1024 + half * 512: cin * 1024 + half * 512 + 512],
                                start=(cin == 0), stop=(cin == 7)),
                                inc="sT" if (cin == 7 and half == 1) else None)
                    P.wait("scalar", "sT", mmv)
                    P.wait("scalar", "sV")
                    sg = P.ins("scalar", lambda e, tc=tc: e.activation(
                        gatebuf, pp0.ap()[:, 0:1024], AF.Silu,
                        scale=sgc[:, tc:tc + 1]), inc="sA")
                    P.wait("vector", "sA", sg)
                    guv[k] = P.ins("vector", lambda e, tc=tc, k=k, iq=iq: e.scalar_tensor_tensor(
                        gu_full[k][:, iq * 1024:(iq + 1) * 1024],
                        pp1.ap()[:, 0:1024] if False else pp0.ap()[:, 1024:2048],
                        suc[:, tc:tc + 1], gatebuf, ALU.mult, ALU.mult), inc="sV")
            for dg in range(2):
                for kk in range(2):
                    k = dg * 2 + kk
                    tc = 4 * g + k
                    P.wait("vector", "sV", guv[k])
                    P.ins("vector", lambda e, k=k: e.reduce_max(
                        v4["h"], gu_full[k], axis=AX.X, apply_absolute_value=True))
                    P.ins("vector", lambda e: e.drain())
                    P.ins("vector", lambda e: e.tensor_single_scalar(v4["i"], v4["h"], 1e-5, ALU.max))
                    P.ins("vector", lambda e: e.drain())
                    P.ins("vector", lambda e: e.reciprocal(v4["j"], v4["i"]))
                    P.ins("vector", lambda e: e.drain())
                    P.ins("vector", lambda e: e.tensor_scalar_mul(v4["k"], v4["j"], 127.0))
                    P.ins("vector", lambda e, kk=kk: e.tensor_scalar_mul(
                        sdc[:, kk:kk + 1], v4["i"], 1.0 / 127.0))
                    P.ins("vector", lambda e: e.drain())
                    P.ins("vector", lambda e, kk=kk: e.tensor_scalar(
                        sdc[:, kk:kk + 1], sdc[:, kk:kk + 1], mwd, None, ALU.mult))
                    for hh in range(2):
                        sl = slice(hh * 2048, hh * 2048 + 2048)
                        P.wait("vector", "sT")
                        iv = P.ins("vector", lambda e, k=k, sl=sl: e.tensor_scalar(
                            pi_t.ap()[:, 0:2048], gu_full[k][:, sl], v4["k"], None,
                            ALU.mult), inc="sV")
                        P.wait("scalar", "sV", iv)
                        rv = P.ins("scalar", lambda e: e.activation(
                            rA, pi_t.ap()[:, 0:2048], AF.Copy, bias=-1.0, scale=1.0), inc="sA")
                        P.ins("vector", lambda e: e.drain())
                        P.ins("vector", lambda e, k=k, sl=sl: e.scalar_tensor_tensor(
                            rB, gu_full[k][:, sl], v4["k"], pi_t.ap()[:, 0:2048],
                            ALU.mult, ALU.subtract))
                        P.wait("vector", "sA", rv)
                        P.ins("vector", lambda e: e.drain())
                        P.ins("vector", lambda e: e.scalar_tensor_tensor(
                            rC, rB, 0.5, rA, ALU.is_ge, ALU.add))
                        P.ins("vector", lambda e: e.drain())
                        nv = P.ins("vector", lambda e: e.scalar_tensor_tensor(
                            nqf4, rB, -0.5, rC, ALU.is_ge, ALU.add), inc="sV")
                        P.wait("tensor", "sV", nv)
                        P.wait("tensor", "sA")
                        tv = None
                        for iic in range(16):
                            tv = P.ins("tensor", lambda e, iic=iic: e.transpose(
                                pp0.ap()[:, iic * 128:iic * 128 + 128],
                                nqf4[:, iic * 128:iic * 128 + 128], idf_a),
                                inc="sT" if iic == 15 else None)
                        P.wait("scalar", "sT", tv)
                        P.ins("scalar", lambda e, hh=hh, kk=kk: e.activation(
                            gu_im.rearrange("p (ic kv t) -> p ic kv t", ic=32, kv=2)
                            [:, hh * 16:hh * 16 + 16, kk, :],
                            pp0.ap().rearrange("p (ic t) -> p ic t", ic=16),
                            AF.Copy), inc="sA")
                P.wait("sync", "sA")
                P.wait("sync", "sT")
                lxo = [None, None]
                for kk in range(2):
                    tc = 4 * g + dg * 2 + kk
                    lxo[kk] = P.dma("sync", lambda e, tc=tc, kk=kk: e.dma_start(
                        out=xo[kk], in_=xown[tc * 128:tc * 128 + 128, :]), "xo%d" % kk)
                gusA = P.cnt("sA")
                P.wait("tensor", "sA", gusA)
                mmd = None
                icmark = {}
                for ic in range(32):
                    sl2 = ic % 2
                    if ic >= 2:
                        P.wait("sync", "sT", icmark[ic - 1])
                    lw = P.dma("sync", lambda e, ic=ic, sl2=sl2: e.dma_start(
                        out=wdq_s[sl2], in_=Dq_r[:, ic, :]), "wd%d" % sl2)
                    P.wait("tensor", "q:wd%d" % sl2, lw)
                    for kk in range(2):
                        for cb in range(2):
                            mmd = P.ins("tensor", lambda e, ic=ic, kk=kk, cb=cb, sl2=sl2: e.matmul(
                                pp0.ap()[:, kk * 1024 + cb * 512: kk * 1024 + cb * 512 + 512],
                                gu_im[:, ic * 256 + kk * 128: ic * 256 + kk * 128 + 128],
                                wdq_s[sl2][:, cb * 512:cb * 512 + 512],
                                start=(ic == 0), stop=(ic == 31)),
                                inc="sT" if (kk == 1 and cb == 1) else None)
                    icmark[ic] = mmd
                P.wait("vector", "sT", mmd)
                P.wait("vector", "q:sp0")
                P.wait("vector", "q:sp1")
                for kk in range(2):
                    tc = 4 * g + dg * 2 + kk
                    P.wait("vector", "q:xo%d" % kk, lxo[kk])
                    pv = P.ins("vector", lambda e, kk=kk: e.scalar_tensor_tensor(
                        poutb[kk], pp0.ap()[:, kk * 1024:kk * 1024 + 1024],
                        sdc[:, kk:kk + 1], xo[kk], ALU.mult, ALU.add), inc="sV")
                    P.wait("gpsimd", "sV", pv)
                    P.dma("gpsimd", lambda e, tc=tc, kk=kk: e.dma_start(
                        out=Pt_r[:, tc, :], in_=poutb[kk]), "sp%d" % kk)
        pf.pop(); ph.pop()
        P.barrier()

        pf.push()
        S = [pf.alloc(4096), pf.alloc(4096)]
        Kb = [pf.alloc(2048), pf.alloc(2048)]
        Vb = [pf.alloc(2048), pf.alloc(2048)]
        Qp = pf.alloc(2048)
        PTs = [pf.alloc(256), pf.alloc(256)]
        Pblk = [pf.alloc(256), pf.alloc(256)]
        msk = [pf.alloc(512), pf.alloc(512)]
        wo_sb = pf.alloc(8192)
        v5 = {k: sm.alloc(1) for k in "mnop"}
        nmv = sm.alloc(2)
        bias_e = sm.alloc(2)
        lrun = sm.alloc(2)
        lsb = sm.alloc(2)
        linv = sm.alloc(2)

        P.dma("sync", lambda e: e.dma_start(
            out=wo_sb.rearrange("p (cc n) -> p cc n", cc=8),
            in_=wo.rearrange("(cc p) n -> p cc n", p=128)), "lw")
        Vs_rd = Vs.rearrange("(n p) c -> p n c", p=128)
        sevm = [0, 0, 0, 0]
        prev_onv = 0
        ptev = [0, 0]
        pvmmk = [0, 0]

        AOfm = S[0][:, 0:2048]
        Ofm = S[1][:, 0:2048]
        onb = S[1][:, 2048:3072]
        ptl = [S[0][:, 2048:3072], S[0][:, 3072:4096]]
        outb = [Kb[0][:, 0:1024], Kb[1][:, 0:1024]]

        for p5 in (range(8) if phases >= 5 else []):
            L = 512 * (p5 + 1)
            nb = L // 256
            o0 = 256 * p5
            P.wait("sync", "sT")
            P.wait("sync", "q:so0")
            P.wait("sync", "q:so1")
            P.wait("scalar", "sV")
            lq = P.dma("sync", lambda e, o0=o0: e.dma_start(
                out=Qp.rearrange("p (cc t) -> p cc t", cc=8),
                in_=Qs.rearrange("(cc p) t -> p cc t", p=128)[:, :, o0:o0 + 256]), "qp")
            P.wait("sync", "sV")
            lm0 = P.dma("sync", lambda e, p5=p5: e.dma_start(
                out=msk[0], in_=maskseq[2 * p5, :, :]), "mk0")
            lm1 = P.dma("sync", lambda e, p5=p5: e.dma_start(
                out=msk[1], in_=maskseq[2 * p5 + 1, :, :]), "mk1")
            evs = [None, None]
            for b in range(nb):
                sl = b % 2
                P.wait("sync", "sT")
                lk = P.dma("sync", lambda e, b=b, sl=sl: e.dma_start(
                    out=Kb[sl].rearrange("p (cc t) -> p cc t", cc=8),
                    in_=Ks_r[:, :, b * 256:b * 256 + 256]), "kb%d" % sl)
                P.wait("tensor", "q:kb%d" % sl, lk)
                if b == 0:
                    P.wait("tensor", "q:qp", lq)
                for qt in range(2):
                    psl = (2 * b + qt) % 4
                    psS = pp0.ap()[:, psl * 512:psl * 512 + 256]
                    P.wait("tensor", "sA", max(sevm[psl], prev_onv))
                    mv = None
                    for cc in range(8):
                        mv = P.ins("tensor", lambda e, cc=cc, qt=qt, sl=sl, psS=psS: e.matmul(
                            psS, Qp[:, cc * 256 + qt * 128: cc * 256 + qt * 128 + 128],
                            Kb[sl][:, cc * 256:cc * 256 + 256],
                            start=(cc == 0), stop=(cc == 7)),
                            inc="sT" if cc == 7 else None)
                    P.wait("scalar", "sT", mv)
                    evs[qt] = P.ins("scalar", lambda e, qt=qt, b=b, psS=psS: e.activation(
                        S[qt][:, b * 256:b * 256 + 256], psS, AF.Copy), inc="sA")
                    sevm[psl] = evs[qt]
            for qt in range(2):
                P.wait("vector", "sA", evs[qt])
                P.wait("vector", "q:mk0", lm0)
                P.wait("vector", "q:mk1", lm1)
                P.ins("vector", lambda e, qt=qt, L=L: e.tensor_tensor(
                    S[qt][:, L - 512:L], S[qt][:, L - 512:L], msk[qt], op=ALU.add))
                P.ins("vector", lambda e: e.drain())
                P.ins("vector", lambda e, qt=qt, L=L: e.reduce_max(
                    nmv[:, qt:qt + 1], S[qt][:, 0:L], axis=AX.X, negate=True))
                P.ins("vector", lambda e: e.drain())
                P.ins("vector", lambda e, qt=qt: e.tensor_scalar_mul(
                    bias_e[:, qt:qt + 1], nmv[:, qt:qt + 1], 1.0 / 32.0), inc="sV")
            mbv = P.cnt("sV")
            if p5stage < 2:
                continue
            lsb_used = [None, None]
            vbmark = [0, 0]
            it5 = 0
            for b in range(nb):
                sl = b % 2
                P.wait("sync", "sT", vbmark[sl])
                lv = P.dma("sync", lambda e, b=b, sl=sl: e.dma_start(
                    out=Vb[sl].rearrange("p (n c) -> p n c", n=2),
                    in_=Vs_rd[:, 2 * b:2 * b + 2, :]), "vb%d" % sl)
                for qt in range(2):
                    pts = it5 % 2
                    it5 += 1
                    P.wait("scalar", "sV", mbv)
                    if lsb_used[qt] is not None:
                        P.wait("scalar", "sV", lsb_used[qt])
                    P.wait("scalar", "sT", pvmmk[1 - pts])
                    ex = P.ins("scalar", lambda e, qt=qt, b=b: e.activation(
                        Pblk[qt], S[qt][:, b * 256:b * 256 + 256], AF.Exp,
                        bias=bias_e[:, qt:qt + 1], scale=1.0 / 32.0,
                        accum_out=lsb[:, qt:qt + 1]), inc="sA")
                    P.wait("vector", "sA", ex)
                    P.ins("vector", lambda e: e.drain())
                    if b == 0:
                        lsb_used[qt] = P.ins("vector", lambda e, qt=qt: e.tensor_copy(
                            lrun[:, qt:qt + 1], lsb[:, qt:qt + 1]), inc="sV")
                    else:
                        lsb_used[qt] = P.ins("vector", lambda e, qt=qt: e.tensor_tensor(
                            lrun[:, qt:qt + 1], lrun[:, qt:qt + 1], lsb[:, qt:qt + 1],
                            op=ALU.add), inc="sV")
                    P.wait("tensor", "sA", max(ex, ptev[pts]))
                    tv = None
                    for ks in range(2):
                        tv = P.ins("tensor", lambda e, ks=ks, qt=qt, pts=pts: e.transpose(
                            pp1.ap()[:, pts * 256 + ks * 128: pts * 256 + ks * 128 + 128],
                            Pblk[qt][:, ks * 128:ks * 128 + 128], idf_a),
                            inc="sT" if ks == 1 else None)
                    P.wait("scalar", "sT", max(tv, pvmmk[pts]))
                    ptev[pts] = P.ins("scalar", lambda e, pts=pts: e.activation(
                        PTs[pts], pp1.ap()[:, pts * 256:pts * 256 + 256], AF.Copy), inc="sA")
                    P.wait("tensor", "sA", ptev[pts])
                    P.wait("tensor", "q:vb%d" % sl, lv)
                    mv = None
                    for ks in range(2):
                        for cb in range(2):
                            mv = P.ins("tensor", lambda e, ks=ks, cb=cb, qt=qt, sl=sl, b=b, nb=nb, pts=pts: e.matmul(
                                pp0.ap()[:, qt * 1024 + cb * 512: qt * 1024 + cb * 512 + 512],
                                PTs[pts][:, ks * 128:ks * 128 + 128],
                                Vb[sl][:, ks * 1024 + cb * 512: ks * 1024 + cb * 512 + 512],
                                start=(b == 0 and ks == 0), stop=(b == nb - 1 and ks == 1)),
                                inc="sT" if (ks == 1 and cb == 1) else None)
                    pvmmk[pts] = mv
                    vbmark[sl] = mv
            pvdone = P.cnt("sT")
            if p5stage < 3:
                continue
            P.ins("vector", lambda e: e.drain())
            for qt in range(2):
                P.ins("vector", lambda e, qt=qt: e.reciprocal(
                    linv[:, qt:qt + 1], lrun[:, qt:qt + 1]))
            lnv5 = P.ins("vector", lambda e: e.drain(), inc="sV")
            for qt in range(2):
                P.wait("scalar", "sV", lnv5)
                P.wait("scalar", "sT", pvdone)
                onv = P.ins("scalar", lambda e, qt=qt: e.activation(
                    onb, pp0.ap()[:, qt * 1024:qt * 1024 + 1024], AF.Identity,
                    scale=linv[:, qt:qt + 1]), inc="sA")
                prev_onv = onv
                P.wait("tensor", "sA", onv)
                P.wait("tensor", "sA")
                P.wait("tensor", "sV")
                tv = None
                for cc in range(8):
                    tv = P.ins("tensor", lambda e, cc=cc: e.transpose(
                        pp1.ap()[:, cc * 128:cc * 128 + 128],
                        onb[:, cc * 128:cc * 128 + 128], idf_a),
                        inc="sT" if cc == 7 else None)
                P.wait("scalar", "sT", tv)
                P.ins("scalar", lambda e, qt=qt: e.activation(
                    Ofm.rearrange("p (cc t) -> p cc t", cc=8)[:, :, qt * 128:qt * 128 + 128],
                    pp1.ap()[:, 0:1024].rearrange("p (cc t) -> p cc t", cc=8),
                    AF.Copy), inc="sA")
            ofmv = P.cnt("sA")
            P.wait("tensor", "sA", ofmv)
            P.wait("tensor", "q:lw")
            aov = None
            for coutc in range(8):
                P.wait("tensor", "sA")
                mv = None
                for cin in range(8):
                    mv = P.ins("tensor", lambda e, coutc=coutc, cin=cin: e.matmul(
                        pp0.ap()[:, coutc * 256:coutc * 256 + 256],
                        wo_sb[:, cin * 1024 + coutc * 128: cin * 1024 + coutc * 128 + 128],
                        Ofm[:, cin * 256:cin * 256 + 256],
                        start=(cin == 0), stop=(cin == 7)),
                        inc="sT" if cin == 7 else None)
                P.wait("scalar", "sT", mv)
                aov = P.ins("scalar", lambda e, coutc=coutc: e.activation(
                    AOfm[:, coutc * 256:coutc * 256 + 256],
                    pp0.ap()[:, coutc * 256:coutc * 256 + 256],
                    AF.Identity, bias=bop_a[:, coutc:coutc + 1]), inc="sA")
            for qt in range(2):
                P.wait("sync", "sA", ofmv)
                lpt = P.dma("sync", lambda e, o0=o0, qt=qt: e.dma_start(
                    out=ptl[qt], in_=Pt[o0 + qt * 128:o0 + qt * 128 + 128, :]), "pl%d" % qt)
                P.wait("tensor", "sA", aov)
                P.wait("tensor", "sA")
                P.wait("tensor", "sV")
                tv = None
                for cc in range(8):
                    tv = P.ins("tensor", lambda e, cc=cc, qt=qt: e.transpose(
                        pp1.ap()[:, cc * 128:cc * 128 + 128],
                        AOfm[:, cc * 256 + qt * 128: cc * 256 + qt * 128 + 128], idf_a),
                        inc="sT" if cc == 7 else None)
                P.wait("vector", "sT", tv)
                P.wait("vector", "q:pl%d" % qt, lpt)
                ov = P.ins("vector", lambda e, qt=qt: e.tensor_tensor(
                    outb[qt], pp1.ap()[:, 0:1024], ptl[qt], op=ALU.add), inc="sV")
                P.wait("gpsimd", "sV", ov)
                P.dma("gpsimd", lambda e, o0=o0, qt=qt: e.dma_start(
                    out=out[o0 + qt * 128:o0 + qt * 128 + 128, :], in_=outb[qt]), "so%d" % qt)
        pf.pop()
        P.wait("gpsimd", "q:so0")
        P.wait("gpsimd", "q:so1")
        P.flush()
    return nc


def _tables():
    import jax
    import jax.numpy as jnp
    with jax.default_device(jax.devices("cpu")[0]):
        inv_freq = 1.0 / THETA ** (jnp.arange(0, C, 2, dtype=jnp.float32) / C)
        freqs = jnp.arange(T, dtype=jnp.float32)[:, None] * inv_freq[None, :]
        freqs = jnp.repeat(freqs, 2, axis=-1)
        ch = np.asarray(jnp.cos(freqs), dtype=np.float32)
        sin = np.asarray(jnp.sin(freqs), dtype=np.float32)
    ss = sin.copy()
    ss[:, 0::2] = -sin[:, 0::2]
    return np.ascontiguousarray(ch), np.ascontiguousarray(ss)


def _masks(rows):
    ms = np.zeros((16, 128, 512), dtype=np.float32)
    ii = np.arange(128)[:, None]
    jj = np.arange(512)[None, :]
    for i, r in enumerate(rows):
        Lb = ((r + 128 + 511) // 512) * 512 - 512
        ms[i] = np.where(r + ii >= Lb + jj, 0.0, NEG)
    return ms


def kernel(L0_attn_x, L0_attn_Wq, L0_attn_bq, L0_attn_Wk, L0_attn_bk,
           L0_attn_Wv, L0_attn_bv, L0_attn_Wo, L0_attn_bo,
           L0_ffn_w_gate, L0_ffn_w_up, L0_ffn_w_down):
    x = np.ascontiguousarray(np.asarray(L0_attn_x, dtype=np.float32))
    ch, ss = _tables()
    ident = np.eye(128, dtype=np.float32)
    ones2 = np.ones((128, 128), dtype=np.float32)
    ones1 = np.ones((1, 128), dtype=np.float32)
    wgT = np.ascontiguousarray(np.asarray(L0_ffn_w_gate, np.float32).T)
    wuT = np.ascontiguousarray(np.asarray(L0_ffn_w_up, np.float32).T)
    wdT = np.ascontiguousarray(np.asarray(L0_ffn_w_down, np.float32).T)

    in_maps = []
    row_maps = []
    for core in range(8):
        b, h = core // 2, core % 2
        J = J0 if h == 0 else J1
        rows = np.concatenate([np.arange(c * 256, c * 256 + 256) for c in J])
        row_maps.append((b, rows))
        qrows = rows[::128]
        m = {
            "xb": np.ascontiguousarray(x[b]),
            "xown": np.ascontiguousarray(x[b][rows]),
            "tkch": ch, "tksh": ss,
            "tqch": np.ascontiguousarray(ch[rows]),
            "tqsh": np.ascontiguousarray(ss[rows]),
            "wq": np.ascontiguousarray(np.asarray(L0_attn_Wq, np.float32)),
            "wk": np.ascontiguousarray(np.asarray(L0_attn_Wk, np.float32)),
            "wv": np.ascontiguousarray(np.asarray(L0_attn_Wv, np.float32)),
            "wo": np.ascontiguousarray(np.asarray(L0_attn_Wo, np.float32)),
            "bqv": np.asarray(L0_attn_bq, np.float32).reshape(1, C),
            "bkv": np.asarray(L0_attn_bk, np.float32).reshape(1, C),
            "bvv": np.asarray(L0_attn_bv, np.float32).reshape(1, C),
            "bov": np.asarray(L0_attn_bo, np.float32).reshape(1, C),
            "bopv": np.ascontiguousarray(np.asarray(L0_attn_bo, np.float32).reshape(8, 128).T),
            "wgT": wgT, "wuT": wuT, "wdT": wdT,
            "maskseq": _masks(qrows),
            "identv": ident, "ones2d": ones2, "ones1r": ones1,
        }
        in_maps.append(m)

    nc = build_nc()
    res = run_bass_kernel_spmd(nc, in_maps, core_ids=list(range(8)))
    full = np.empty((B, T, C), dtype=np.float32)
    for core in range(8):
        b, rows = row_maps[core]
        full[b, rows] = res.results[core]["out"]
    return full


